# version 03
# baseline (speedup 1.0000x reference)
import numpy as np
import ml_dtypes
from contextlib import ExitStack

import concourse.bass as bass
import concourse.mybir as mybir
from concourse.bass_utils import run_bass_kernel_spmd

F32 = mybir.dt.float32
BF16 = mybir.dt.bfloat16
AF = mybir.ActivationFunctionType
ALU = mybir.AluOpType
BF = ml_dtypes.bfloat16

D = 4096
DCH = 32
SEQ = 4096
HD = 128
NH = 32
NKV = 8
HID = 11008
HCH = 86
NG = 4
GSZ = 512
OWN = NG * GSZ
CNT = [32, 24, 16, 8]
ISCALE = 0.08838834764831845
EPS = 1e-5

DEBUG = False
PHASES = "ALL"


class SemCtr:
    __slots__ = ("h", "n")

    def __init__(self, h):
        self.h = h
        self.n = 0


def _w(eng, sem, val):
    if val > 0:
        eng.wait_ge(sem.h, val)


def build_nc():
    nc = bass.Bass("TRN2", target_bir_lowering=False)

    def din(name, shape, dtype=F32):
        return nc.dram_tensor(name, shape, dtype, kind="ExternalInput").ap()

    def dscr(name, shape, dtype):
        kind = "ExternalOutput" if DEBUG else "Internal"
        return nc.dram_tensor(name, shape, dtype, kind=kind).ap()

    x = din("x", [SEQ, D])
    xq = din("xq", [OWN, D])
    xqt = din("xqt", [D, OWN])
    qcos = din("qcos", [128, OWN]); qsin = din("qsin", [128, OWN])
    kcos = din("kcos", [128, SEQ]); ksin = din("ksin", [128, SEQ])
    msk = din("msk", [NG, 8, 128, GSZ], BF16)
    idbf = din("idbf", [128, 128], BF16)
    idf32 = din("idf32", [128, 128], F32)
    wq = din("wq", [NH, 128, DCH, 128])
    wk = din("wk", [NKV, 128, DCH, 128])
    wv = din("wv", [4, 128, DCH, 256])
    wo = din("wo", [DCH, 128, DCH, 128])
    gw = din("gw", [HCH, 128, DCH, 128])
    uw = din("uw", [HCH, 128, DCH, 128])
    dw = din("dw", [DCH, 128, HCH, 128])
    gua = din("gua", [128, DCH, 32])
    gb = din("gb", [16, HCH, 128])
    ub = din("ub", [16, HCH, 128])
    da = din("da", [128, HCH, 16])
    db = din("db", [16, DCH, 128])
    out_t = nc.dram_tensor("out_t", [D, OWN], F32, kind="ExternalOutput").ap()

    k_scr = dscr("k_scr", [NKV, 128, SEQ], BF16)
    v_scr = dscr("v_scr", [NKV, SEQ, HD], BF16)
    q_scr = dscr("q_scr", [NH, 128, OWN], BF16)
    xn_scr = dscr("xn_scr", [DCH, 128, OWN], BF16)
    s_scr = dscr("s_scr", [DCH, 128, OWN], F32)
    gw_scr = nc.dram_tensor("gw_scr", [HCH, 128, DCH * 128], BF16,
                            kind="ExternalOutput" if DEBUG else "Internal").ap()
    uw_scr = nc.dram_tensor("uw_scr", [HCH, 128, DCH * 128], BF16, kind="Internal").ap()
    dw_scr = nc.dram_tensor("dw_scr", [DCH, 128, HCH * 128], BF16, kind="Internal").ap()

    es = ExitStack()
    nc._kernel_es = es

    _uid = [0]
    _orig_sbuf = nc.sbuf_tensor
    _orig_psum = nc.psum_tensor

    def _usb(name, shape, dtype, **kw):
        _uid[0] += 1
        return _orig_sbuf(f"{name}_{_uid[0]}", shape, dtype, **kw)

    def _ups(name, shape, dtype, **kw):
        _uid[0] += 1
        return _orig_psum(f"{name}_{_uid[0]}", shape, dtype, **kw)

    nc.sbuf_tensor = _usb
    nc.psum_tensor = _ups

    S = {n: SemCtr(es.enter_context(nc.semaphore(n))) for n in [
        "ini", "xld", "ln", "rs", "xn", "tp", "ev", "xts", "xtw", "cs",
        "wK", "mmK", "ropS", "ropV", "kvw",
        "wQ", "mmQ", "qw", "xnw",
        "kvL", "qL", "mL", "Smm", "Ex", "Mk", "Pc", "rc", "bc", "lb", "at",
        "woL", "womm", "sev", "sw",
        "xnl", "fw", "lwl", "axe", "gmm", "umm", "sil", "hid",
        "dwl", "dmm", "axd2", "gs", "sx", "cmb", "outw", "p0l", "p0s",
    ]}

    id_bf = es.enter_context(nc.sbuf_tensor("id_bf", [128, 128], BF16))
    id_f32 = es.enter_context(nc.sbuf_tensor("id_f32", [128, 128], F32))
    ones_bf = es.enter_context(nc.sbuf_tensor("ones_bf", [128, 1], BF16))
    onesr_f = es.enter_context(nc.sbuf_tensor("onesr_f", [1, 128], F32))

    with nc.Block() as blk:
        @blk.gpsimd
        def _(g):
            g.dma_start(id_bf[:, :], idbf).then_inc(S["ini"].h, 16)
            g.dma_start(id_f32[:, :], idf32).then_inc(S["ini"].h, 16)
            g.memset(ones_bf[:, :], 1.0)
            g.memset(onesr_f[:, :], 1.0)
            g.wait_ge(S["ini"].h, 32)
    S["ini"].n = 32

    cast_jobs = ([("g", h) for h in range(HCH)] + [("u", h) for h in range(HCH)]
                 + [("d", dm) for dm in range(DCH)])

    cast_state = {"cursor": 0}

    def emit_one_cast(g_eng, bounce, fresh):
        j = cast_state["cursor"]
        if j >= len(cast_jobs):
            return
        cast_state["cursor"] = j + 1
        kind, i = cast_jobs[j]
        W = HCH * 128
        bb = (j % 2) * W
        cols = DCH * 128 if kind in ("g", "u") else HCH * 128
        sa = {"g": gw, "u": uw, "d": dw}[kind][i]
        da = {"g": gw_scr, "u": uw_scr, "d": dw_scr}[kind][i]
        if fresh:
            _w(g_eng, S["p0s"], S["p0s"].n)
        else:
            _w(g_eng, S["p0s"], 16 * (j - 1))
        g_eng.dma_start(bounce[:, bb:bb + cols], sa.rearrange("p c m -> p (c m)")
                        ).then_inc(S["p0l"].h, 16)
        g_eng.wait_ge(S["p0l"].h, 16 * (j + 1))
        g_eng.dma_start(da, bounce[:, bb:bb + cols]).then_inc(S["p0s"].h, 16)
        S["p0l"].n += 16
        S["p0s"].n += 16


    def ln_block(src_ap, row0, nrt, xnt, tpc):
        b_xld = S["xld"].n; b_ln = S["ln"].n; b_rs = S["rs"].n; b_xn = S["xn"].n
        b_tp = S["tp"].n; b_ev = S["ev"].n
        with (
            nc.sbuf_tensor("xrow", [128, 2 * D], F32) as xrow,
            nc.sbuf_tensor("stats", [128, 48], F32) as stats,
            nc.sbuf_tensor("mv", [128, 2], F32) as mv,
            nc.sbuf_tensor("veps", [128, 1], F32) as veps,
            nc.sbuf_tensor("lnv", [128, 1], F32) as lnv,
            nc.sbuf_tensor("rstd", [128, 1], F32) as rstd,
            nc.sbuf_tensor("xnbf", [128, 2 * D], BF16) as xnbf,
            nc.sbuf_tensor("dmyv", [128, 512], F32) as dmyv,
            nc.psum_tensor("psT", [128, 2 * 1024], BF16) as psT,
            nc.Block() as blk,
        ):
            b_prev = [(S[n], S[n].n) for n in ("kvw", "qw", "xtw", "p0s")]

            @blk.sync
            def _(sy):
                for sem, val in b_prev:
                    _w(sy, sem, val)
                for rt in range(nrt):
                    _w(sy, S["xn"], b_xn + rt - 1)
                    sy.dma_start(xrow[:, (rt % 2) * D:(rt % 2 + 1) * D],
                                 src_ap[row0 + rt * 128: row0 + (rt + 1) * 128, :]
                                 ).then_inc(S["xld"].h, 16)
            S["xld"].n += 16 * nrt

            @blk.vector
            def _(v):
                for rt in range(nrt):
                    b = (rt % 2) * D
                    _w(v, S["xld"], b_xld + 16 * (rt + 1))
                    for ch in range(8):
                        v.bn_stats(stats[:, 6 * ch:6 * ch + 6],
                                   xrow[:, b + 512 * ch: b + 512 * (ch + 1)])
                    v.bn_aggr(mv[:, :], stats[:, :])
                    v.tensor_tensor(dmyv[:, :], xrow[:, b:b + 512],
                                    xrow[:, b:b + 512], ALU.mult)
                    v.tensor_scalar_add(veps[:, :], mv[:, 1:2], EPS
                                        ).then_inc(S["ln"].h, 1)
                    _w(v, S["rs"], b_rs + rt + 1)
                    _w(v, S["tp"], b_tp + 4 * (rt - 1))
                    v.tensor_scalar(xnbf[:, b:b + D], xrow[:, b:b + D],
                                    mv[:, 0:1], rstd[:, :],
                                    ALU.subtract, ALU.mult).then_inc(S["xn"].h, 1)
            S["ln"].n += nrt
            S["xn"].n += nrt

            xnt_v = xnt[:, :].rearrange("p (c t) -> p c t", c=DCH)

            def emit_evac(s, rt):
                for grp in range(4):
                    u = rt * 4 + grp
                    c0 = 8 * grp
                    _w(s, S["tp"], b_tp + u + 1)
                    s.copy(xnt_v[:, c0:c0 + 8, rt * 128: rt * 128 + 128],
                           psT[:, (u % 2) * 1024:(u % 2 + 1) * 1024].rearrange(
                               "p (c t) -> p c t", c=8)).then_inc(S["ev"].h, 1)

            @blk.scalar
            def _(s):
                for rt in range(nrt):
                    _w(s, S["ln"], b_ln + rt + 1)
                    s.activation(lnv[:, :], veps[:, :], AF.Ln)
                    s.activation(rstd[:, :], lnv[:, :], AF.Exp,
                                 scale=-0.5).then_inc(S["rs"].h, 1)
                    if rt >= 1:
                        emit_evac(s, rt - 1)
                emit_evac(s, nrt - 1)
            S["rs"].n += nrt
            S["ev"].n += 4 * nrt

            @blk.tensor
            def _(t):
                _w(t, S["ini"], 32)
                for rt in range(nrt):
                    b = (rt % 2) * D
                    _w(t, S["xn"], b_xn + rt + 1)
                    for grp in range(4):
                        u = rt * 4 + grp
                        _w(t, S["ev"], b_ev + (u - 1))
                        for j in range(8):
                            c = 8 * grp + j
                            mmT = t.transpose(
                                psT[:, (u % 2) * 1024 + j * 128:
                                    (u % 2) * 1024 + (j + 1) * 128],
                                xnbf[:, b + c * 128:b + (c + 1) * 128], id_bf[:, :])
                        mmT.then_inc(S["tp"].h, 1)
            S["tp"].n += 4 * nrt

    def phase_k(half, xnt):
        hbase = half * 2048
        b = {n: S[n].n for n in ("wK", "mmK", "ropS", "ropV", "kvw", "cs")}
        xnt_v = xnt[:, :].rearrange("p (c t) -> p c t", c=DCH)
        with (
            nc.sbuf_tensor("ccos", [128, 2048], F32) as ccos,
            nc.sbuf_tensor("csin", [128, 2048], F32) as csin,
            nc.sbuf_tensor("wkb", [128, 2 * DCH * 128], BF16) as wkb,
            nc.sbuf_tensor("wvb", [128, DCH * 256], BF16) as wvb,
            nc.sbuf_tensor("krf", [128, 2 * 512], F32) as krf,
            nc.sbuf_tensor("ksf", [128, 2 * 512], F32) as ksf,
            nc.sbuf_tensor("rt1", [128, 512], F32) as rt1,
            nc.sbuf_tensor("rt2", [128, 512], F32) as rt2,
            nc.sbuf_tensor("kfin", [128, 2 * 512], BF16) as kfin,
            nc.sbuf_tensor("vstg", [128, 2 * 256], BF16) as vstg,
            nc.psum_tensor("psK", [128, 2 * 512], F32) as psK,
            nc.psum_tensor("psV", [128, 2 * 512], F32) as psV,
            nc.Block() as blk,
        ):
            @blk.sync
            def _(sy):
                sy.dma_start(ccos[:, :], kcos[:, hbase:hbase + 2048]
                             ).then_inc(S["cs"].h, 16)
                sy.dma_start(csin[:, :], ksin[:, hbase:hbase + 2048]
                             ).then_inc(S["cs"].h, 16)
                for kv in range(NKV):
                    for tg in range(4):
                        u = kv * 4 + tg
                        _w(sy, S["ropV"], b["ropV"] + u + 1)
                        sy.dma_start(
                            k_scr[kv, :, hbase + tg * 512: hbase + (tg + 1) * 512],
                            kfin[:, (u % 2) * 512:(u % 2 + 1) * 512]
                        ).then_inc(S["kvw"].h, 16)
                for hq in range(4):
                    for tt in range(16):
                        u2 = hq * 16 + tt
                        _w(sy, S["ropS"], b["ropS"] + 32 + u2 + 1)
                        tok0 = hbase + tt * 128
                        sy.dma_start(
                            v_scr[2 * hq:2 * hq + 2, tok0:tok0 + 128, :].rearrange(
                                "h p m -> p h m"),
                            vstg[:, (u2 % 2) * 256:(u2 % 2 + 1) * 256].rearrange(
                                "p (h m) -> p h m", h=2)
                        ).then_inc(S["kvw"].h, 16)
            S["cs"].n += 32
            S["kvw"].n += 16 * (32 + 64)

            @blk.gpsimd
            def _(g):
                for kv in range(NKV):
                    _w(g, S["mmK"], b["mmK"] + 4 * (kv - 1))
                    g.dma_start(wkb[:, :].rearrange("p (z c m) -> p z c m", z=2, c=DCH)
                                [:, kv % 2], wk[kv]).then_inc(S["wK"].h, 16)
                for hq in range(4):
                    _w(g, S["mmK"], b["mmK"] + 32 + 16 * hq)
                    g.dma_start(wvb[:, :].rearrange("p (c m) -> p c m", c=DCH),
                                wv[hq]).then_inc(S["wK"].h, 16)
            S["wK"].n += 16 * 12

            @blk.tensor
            def _(t):
                for kv in range(NKV):
                    _w(t, S["wK"], b["wK"] + 16 * (kv + 1))
                    for tg in range(4):
                        u = kv * 4 + tg
                        _w(t, S["ropS"], b["ropS"] + (u - 1))
                        for c in range(DCH):
                            mm = t.matmul(
                                psK[:, (u % 2) * 512:(u % 2 + 1) * 512],
                                wkb[:, (kv % 2) * DCH * 128 + c * 128:
                                    (kv % 2) * DCH * 128 + (c + 1) * 128],
                                xnt_v[:, c, tg * 512:(tg + 1) * 512],
                                start=(c == 0), stop=(c == DCH - 1))
                        mm.then_inc(S["mmK"].h, 1)
                for hq in range(4):
                    _w(t, S["wK"], b["wK"] + 16 * (8 + hq + 1))
                    for tt in range(16):
                        u2 = hq * 16 + tt
                        _w(t, S["ropS"], b["ropS"] + 32 + (u2 - 1))
                        for c in range(DCH):
                            mm = t.matmul(
                                psV[:, (u2 % 2) * 512:(u2 % 2) * 512 + 256],
                                xnt_v[:, c, tt * 128:(tt + 1) * 128],
                                wvb[:, c * 256:(c + 1) * 256],
                                start=(c == 0), stop=(c == DCH - 1))
                        mm.then_inc(S["mmK"].h, 1)
            S["mmK"].n += 32 + 64

            @blk.scalar
            def _(s):
                for kv in range(NKV):
                    for tg in range(4):
                        u = kv * 4 + tg
                        sb = (u % 2) * 512
                        _w(s, S["mmK"], b["mmK"] + u + 1)
                        _w(s, S["ropV"], b["ropV"] + (u - 1))
                        s.copy(krf[:, sb:sb + 512], psK[:, sb:sb + 512])
                        s.copy(ksf[0:64, sb:sb + 512], psK[64:128, sb:sb + 512])
                        s.copy(ksf[64:128, sb:sb + 512], psK[0:64, sb:sb + 512]
                               ).then_inc(S["ropS"].h, 1)
                for hq in range(4):
                    for tt in range(16):
                        u2 = hq * 16 + tt
                        sb = (u2 % 2) * 256
                        _w(s, S["mmK"], b["mmK"] + 32 + u2 + 1)
                        _w(s, S["kvw"], b["kvw"] + 16 * (32 + u2 - 1))
                        s.copy(vstg[:, sb:sb + 256],
                               psV[:, (u2 % 2) * 512:(u2 % 2) * 512 + 256]
                               ).then_inc(S["ropS"].h, 1)
            S["ropS"].n += 32 + 64

            @blk.vector
            def _(v):
                _w(v, S["cs"], b["cs"] + 32)
                for kv in range(NKV):
                    for tg in range(4):
                        u = kv * 4 + tg
                        sb = (u % 2) * 512
                        _w(v, S["ropS"], b["ropS"] + u + 1)
                        _w(v, S["kvw"], b["kvw"] + 16 * (u - 1))
                        cs_ = slice(tg * 512, (tg + 1) * 512)
                        v.tensor_tensor(rt1[:, :], ksf[:, sb:sb + 512],
                                        csin[:, cs_], ALU.mult)
                        v.tensor_tensor(rt2[:, :], krf[:, sb:sb + 512],
                                        ccos[:, cs_], ALU.mult)
                        v.tensor_tensor(kfin[:, sb:sb + 512], rt1[:, :], rt2[:, :],
                                        ALU.add).then_inc(S["ropV"].h, 1)
            S["ropV"].n += 32

    def phase_q(oh, xnt_own, cast_range=None):
        b = {n: S[n].n for n in ("wQ", "mmQ", "ropS", "ropV", "qw", "cs", "xnw")}
        xnt_v = xnt_own[:, :].rearrange("p (c t) -> p c t", c=DCH)
        b_prevq = [(S[n], S[n].n) for n in ("xtw", "xnw", "qw", "p0s")]
        with (
            nc.sbuf_tensor("qcosb", [128, 1024], F32) as qcosb,
            nc.sbuf_tensor("qsinb", [128, 1024], F32) as qsinb,
            nc.sbuf_tensor("wqb", [128, 2 * DCH * 128], BF16) as wqb,
            nc.sbuf_tensor("qrf", [128, 2 * 512], F32) as qrf,
            nc.sbuf_tensor("qsf", [128, 2 * 512], F32) as qsf,
            nc.sbuf_tensor("qt1", [128, 512], F32) as qt1,
            nc.sbuf_tensor("qt2", [128, 512], F32) as qt2,
            nc.sbuf_tensor("qfin", [128, 2 * 512], BF16) as qfin,
            nc.sbuf_tensor("cbounce", [128, 2 * HCH * 128], BF16) as cbounce,
            nc.psum_tensor("psQ", [128, 2 * 512], F32) as psQ,
            nc.Block() as blk,
        ):
            @blk.sync
            def _(sy):
                for sem, val in b_prevq:
                    _w(sy, sem, val)
                sy.dma_start(qcosb[:, :], qcos[:, oh * 1024:(oh + 1) * 1024]
                             ).then_inc(S["cs"].h, 16)
                sy.dma_start(qsinb[:, :], qsin[:, oh * 1024:(oh + 1) * 1024]
                             ).then_inc(S["cs"].h, 16)
                sy.dma_start(
                    xn_scr[:, :, oh * 1024:(oh + 1) * 1024].rearrange("c p t -> p c t"),
                    xnt_own[:, :].rearrange("p (c t) -> p c t", c=DCH)
                ).then_inc(S["xnw"].h, 16)
                for qh in range(NH):
                    for lg in range(2):
                        u = qh * 2 + lg
                        _w(sy, S["ropV"], b["ropV"] + u + 1)
                        g_ = 2 * oh + lg
                        sy.dma_start(
                            q_scr[qh, :, g_ * 512:(g_ + 1) * 512],
                            qfin[:, (u % 2) * 512:(u % 2 + 1) * 512]
                        ).then_inc(S["qw"].h, 16)
            S["cs"].n += 32
            S["xnw"].n += 16
            S["qw"].n += 16 * 64

            @blk.gpsimd
            def _(g):
                for sem, val in b_prevq:
                    _w(g, sem, val)
                for qh in range(NH):
                    _w(g, S["mmQ"], b["mmQ"] + 2 * (qh - 1))
                    g.dma_start(wqb[:, :].rearrange("p (z c m) -> p z c m", z=2, c=DCH)
                                [:, qh % 2], wq[qh]).then_inc(S["wQ"].h, 16)
                    if cast_range is not None and qh < cast_range[1] - cast_range[0]:
                        emit_one_cast(g, cbounce, fresh=(qh < 2))
            S["wQ"].n += 16 * NH

            @blk.tensor
            def _(t):
                for qh in range(NH):
                    _w(t, S["wQ"], b["wQ"] + 16 * (qh + 1))
                    for lg in range(2):
                        u = qh * 2 + lg
                        _w(t, S["ropS"], b["ropS"] + (u - 1))
                        for c in range(DCH):
                            mm = t.matmul(
                                psQ[:, (u % 2) * 512:(u % 2 + 1) * 512],
                                wqb[:, (qh % 2) * DCH * 128 + c * 128:
                                    (qh % 2) * DCH * 128 + (c + 1) * 128],
                                xnt_v[:, c, lg * 512:(lg + 1) * 512],
                                start=(c == 0), stop=(c == DCH - 1))
                        mm.then_inc(S["mmQ"].h, 1)
            S["mmQ"].n += 64

            @blk.scalar
            def _(s):
                for qh in range(NH):
                    for lg in range(2):
                        u = qh * 2 + lg
                        sb = (u % 2) * 512
                        _w(s, S["mmQ"], b["mmQ"] + u + 1)
                        _w(s, S["ropV"], b["ropV"] + (u - 1))
                        s.copy(qrf[:, sb:sb + 512], psQ[:, sb:sb + 512])
                        s.copy(qsf[0:64, sb:sb + 512], psQ[64:128, sb:sb + 512])
                        s.copy(qsf[64:128, sb:sb + 512], psQ[0:64, sb:sb + 512]
                               ).then_inc(S["ropS"].h, 1)
            S["ropS"].n += 64

            @blk.vector
            def _(v):
                _w(v, S["cs"], b["cs"] + 32)
                for qh in range(NH):
                    for lg in range(2):
                        u = qh * 2 + lg
                        sb = (u % 2) * 512
                        _w(v, S["ropS"], b["ropS"] + u + 1)
                        _w(v, S["qw"], b["qw"] + 16 * (u - 1))
                        cs_ = slice(lg * 512, (lg + 1) * 512)
                        v.tensor_tensor(qt1[:, :], qsf[:, sb:sb + 512],
                                        qsinb[:, cs_], ALU.mult)
                        v.tensor_tensor(qt2[:, :], qrf[:, sb:sb + 512],
                                        qcosb[:, cs_], ALU.mult)
                        v.tensor_tensor(qfin[:, sb:sb + 512], qt1[:, :], qt2[:, :],
                                        ALU.add).then_inc(S["ropV"].h, 1)
            S["ropV"].n += 64

    def phase_a():
        b = {n: S[n].n for n in (
            "kvL", "qL", "mL", "Smm", "Ex", "Mk", "Pc", "rc", "bc", "lb",
            "at", "woL", "womm", "sev", "sw", "kvw", "qw", "xnw", "xtw")}

        sched = []
        for g in range(NG):
            npair = CNT[g] // 2
            for kv in range(NKV):
                for q4 in range(4):
                    ih = (g * 8 + kv) * 4 + q4
                    for i in range(npair):
                        sched.append((g, kv, q4, ih, i, i >= npair - 4))
        ex_at = {}
        mk_at = {}
        ne = b["Ex"]; nm = b["Mk"]
        for idx, rec in enumerate(sched):
            ne += 1
            ex_at[idx] = ne
            if rec[5]:
                nm += 1
                mk_at[idx] = nm
        head_first_pair = {}
        cacc = 0
        for g in range(NG):
            npair = CNT[g] // 2
            for kv in range(NKV):
                for q4 in range(4):
                    ih = (g * 8 + kv) * 4 + q4
                    head_first_pair[ih] = cacc
                    cacc += npair
        NPAIR = cacc

        def pc_through_head(ih):
            if ih < 0:
                return 0
            return head_first_pair[ih] + CNT[ih // 32] // 2

        def pc_through_kv(ikv):
            if ikv < 0:
                return 0
            return pc_through_head(ikv * 4 + 3)

        mk_cum = [0]
        for g in range(NG):
            mk_cum.append(mk_cum[-1] + 4 * 32)

        with (
            nc.sbuf_tensor("ksb", [128, 2 * 4096], BF16) as ksb,
            nc.sbuf_tensor("vsb", [128, 2 * 4096], BF16) as vsb,
            nc.sbuf_tensor("qsb", [128, 2 * 512], BF16) as qsb,
            nc.sbuf_tensor("psbS", [128, 2 * 1024], BF16) as psbS,
            nc.sbuf_tensor("pmS", [128, 2 * 1024], BF16) as pmS,
            nc.sbuf_tensor("msb", [128, 8 * 512], BF16) as msb,
            nc.sbuf_tensor("attnT", [128, NH * 512], BF16) as attnT,
            nc.sbuf_tensor("linv", [1, 512], F32) as linv,
            nc.sbuf_tensor("lbc", [128, 2 * 512], F32) as lbc,
            nc.sbuf_tensor("wob", [128, 2 * DCH * 128], BF16) as wob,
            nc.sbuf_tensor("sst", [128, 2 * 512], F32) as sst,
            nc.sbuf_tensor("abounce", [128, 2 * HCH * 128], BF16) as abounce,
            nc.psum_tensor("psS", [128, 2 * 1024], F32) as psS,
            nc.psum_tensor("psL", [1, 512], F32) as psL,
            nc.psum_tensor("psPV", [128, 2 * 512], F32) as psPV,
            nc.psum_tensor("psB", [128, 512], F32) as psB,
            nc.Block() as blk,
        ):
            @blk.sync
            def _(sy):
                _w(sy, S["kvw"], b["kvw"])
                _w(sy, S["qw"], b["qw"])
                _w(sy, S["xnw"], b["xnw"])
                _w(sy, S["xtw"], b["xtw"])
                _w(sy, S["p0s"], S["p0s"].n)
                for g in range(NG):
                    cnt = CNT[g]
                    _w(sy, S["Mk"], b["Mk"] + mk_cum[g])
                    sy.dma_start(msb[:, :].rearrange("p (i f) -> p i f", i=8),
                                 msk[g].rearrange("i p f -> p i f")
                                 ).then_inc(S["mL"].h, 16)
                    for kv in range(NKV):
                        ikv = g * 8 + kv
                        _w(sy, S["Pc"], b["Pc"] + pc_through_kv(ikv - 2))
                        sy.dma_start(ksb[:, (ikv % 2) * 4096:
                                         (ikv % 2) * 4096 + cnt * 128],
                                     k_scr[kv, :, 0:cnt * 128]).then_inc(S["kvL"].h, 16)
                        sy.dma_start(vsb[:, (ikv % 2) * 4096:
                                         (ikv % 2) * 4096 + cnt * 128]
                                     .rearrange("p (t m) -> p t m", m=128),
                                     v_scr[kv, 0:cnt * 128, :].rearrange(
                                         "(t p) m -> p t m", p=128)
                                     ).then_inc(S["kvL"].h, 16)
                        for q4 in range(4):
                            ih = ikv * 4 + q4
                            qh = kv * 4 + q4
                            _w(sy, S["Smm"], b["Smm"] + head_first_pair.get(ih - 1, 0))
                            sy.dma_start(qsb[:, (ih % 2) * 512:(ih % 2 + 1) * 512],
                                         q_scr[qh, :, g * 512:(g + 1) * 512]
                                         ).then_inc(S["qL"].h, 16)
                    for dm in range(DCH):
                        iw = g * 32 + dm
                        _w(sy, S["sev"], b["sev"] + iw + 1)
                        sy.dma_start(s_scr[dm, :, g * 512:(g + 1) * 512],
                                     sst[:, (iw % 2) * 512:(iw % 2 + 1) * 512]
                                     ).then_inc(S["sw"].h, 16)
            S["mL"].n += 16 * NG
            S["kvL"].n += 32 * 32
            S["qL"].n += 16 * 128
            S["sw"].n += 16 * 128

            @blk.gpsimd
            def _(g_eng):
                for n in ("kvw", "qw", "xnw", "xtw"):
                    _w(g_eng, S[n], b[n])
                for g in range(NG):
                    for dm in range(DCH):
                        iw = g * 32 + dm
                        _w(g_eng, S["womm"], b["womm"] + iw - 1)
                        g_eng.dma_start(
                            wob[:, :].rearrange("p (z c m) -> p z c m", z=2, c=DCH)
                            [:, iw % 2], wo[dm]).then_inc(S["woL"].h, 16)
                        for _k in range(2):
                            emit_one_cast(g_eng, abounce, fresh=(iw < 1))
            S["woL"].n += 16 * 128

            @blk.tensor
            def _(t):
                def lpv(g, ih, kb, npair, i, pb, idx_):
                    masked = (i >= npair - 4)
                    if masked:
                        _w(t, S["Mk"], mk_at[idx_])
                    else:
                        _w(t, S["Ex"], ex_at[idx_])
                    if i == 0:
                        _w(t, S["rc"], b["rc"] + ih)
                        _w(t, S["at"], b["at"] + ih - 1)
                    src = pmS if masked else psbS
                    mm = None
                    for hf in range(2):
                        tt = 2 * i + hf
                        colsrc = src[:, pb * 1024 + hf * 512:
                                     pb * 1024 + (hf + 1) * 512]
                        t.matmul(psL[:, :], ones_bf[:, :], colsrc,
                                 start=(i == 0 and hf == 0), stop=False)
                        mm = t.matmul(psPV[:, (ih % 2) * 512:(ih % 2 + 1) * 512],
                                      vsb[:, kb + tt * 128:kb + (tt + 1) * 128],
                                      colsrc,
                                      start=(i == 0 and hf == 0),
                                      stop=(i == npair - 1 and hf == 1))
                    mm.then_inc(S["Pc"].h, 1)

                idx = 0
                for g in range(NG):
                    cnt = CNT[g]
                    npair = cnt // 2
                    if g > 0:
                        _w(t, S["womm"], b["womm"] + 32 * g)
                        _w(t, S["sev"], b["sev"] + 32 * g)
                    for kv in range(NKV):
                        ikv = g * 8 + kv
                        _w(t, S["kvL"], b["kvL"] + 32 * (ikv + 1))
                        kb = (ikv % 2) * 4096
                        for q4 in range(4):
                            ih = ikv * 4 + q4
                            qb = (ih % 2) * 512
                            _w(t, S["qL"], b["qL"] + 16 * (ih + 1))
                            pend = []
                            for i in range(npair):
                                pb = idx % 2
                                _w(t, S["Ex"], ex_at.get(idx - 2, 0))
                                if idx >= 2 and sched[idx - 2][5]:
                                    _w(t, S["Mk"], mk_at[idx - 2])
                                t.matmul(psS[:, pb * 1024:pb * 1024 + 512],
                                         ksb[:, kb + (2 * i) * 128:
                                             kb + (2 * i + 1) * 128],
                                         qsb[:, qb:qb + 512], start=True, stop=True)
                                t.matmul(psS[:, pb * 1024 + 512:pb * 1024 + 1024],
                                         ksb[:, kb + (2 * i + 1) * 128:
                                             kb + (2 * i + 2) * 128],
                                         qsb[:, qb:qb + 512], start=True,
                                         stop=True).then_inc(S["Smm"].h, 1)
                                pend.append((i, pb, idx))
                                idx += 1
                                if len(pend) >= 2:
                                    lpv(g, ih, kb, npair, *pend.pop(0))
                            while pend:
                                lpv(g, ih, kb, npair, *pend.pop(0))
                            _w(t, S["rc"], b["rc"] + ih + 1)
                            _w(t, S["lb"], b["lb"] + ih)
                            t.matmul(psB[:, :], onesr_f[:, :], linv[:, :],
                                     start=True, stop=True).then_inc(S["bc"].h, 1)
                    _w(t, S["at"], b["at"] + 32 * (g + 1))
                    for dm in range(DCH):
                        iw = g * 32 + dm
                        _w(t, S["woL"], b["woL"] + 16 * (iw + 1))
                        _w(t, S["sev"], b["sev"] + iw - 1)
                        for c in range(DCH):
                            mm = t.matmul(
                                psPV[:, (iw % 2) * 512:(iw % 2 + 1) * 512],
                                wob[:, (iw % 2) * DCH * 128 + c * 128:
                                    (iw % 2) * DCH * 128 + (c + 1) * 128],
                                attnT[:, c * 512:(c + 1) * 512],
                                start=(c == 0), stop=(c == DCH - 1))
                        mm.then_inc(S["womm"].h, 1)
            S["Smm"].n += NPAIR
            S["Pc"].n += NPAIR
            S["bc"].n += 128
            S["womm"].n += 128

            @blk.scalar
            def _(s):
                idx = 0
                for g in range(NG):
                    npair = CNT[g] // 2
                    for kv in range(NKV):
                        for q4 in range(4):
                            ih = (g * 8 + kv) * 4 + q4
                            for i in range(npair):
                                pb = idx % 2
                                _w(s, S["Smm"], b["Smm"] + idx + 1)
                                if idx >= 2:
                                    if sched[idx - 2][5]:
                                        _w(s, S["Mk"], mk_at[idx - 2])
                                    else:
                                        _w(s, S["Pc"], b["Pc"] + idx - 1)
                                s.activation(psbS[:, pb * 1024:(pb + 1) * 1024],
                                             psS[:, pb * 1024:(pb + 1) * 1024],
                                             AF.Exp, scale=ISCALE
                                             ).then_inc(S["Ex"].h, 1)
                                idx += 1
                            _w(s, S["bc"], b["bc"] + ih + 1)
                            _w(s, S["at"], b["at"] + ih - 1)
                            s.copy(lbc[:, (ih % 2) * 512:(ih % 2 + 1) * 512],
                                   psB[:, :]).then_inc(S["lb"].h, 1)
                    for dm in range(DCH):
                        iw = g * 32 + dm
                        _w(s, S["womm"], b["womm"] + iw + 1)
                        _w(s, S["sw"], b["sw"] + 16 * (iw - 1))
                        s.copy(sst[:, (iw % 2) * 512:(iw % 2 + 1) * 512],
                               psPV[:, (iw % 2) * 512:(iw % 2 + 1) * 512]
                               ).then_inc(S["sev"].h, 1)
            S["Ex"].n += NPAIR
            S["lb"].n += 128
            S["sev"].n += 128

            @blk.vector
            def _(v):
                idx = 0
                for g in range(NG):
                    npair = CNT[g] // 2
                    _w(v, S["mL"], b["mL"] + 16 * (g + 1))
                    for kv in range(NKV):
                        for q4 in range(4):
                            ih = (g * 8 + kv) * 4 + q4
                            for i in range(npair):
                                pb = idx % 2
                                if i >= npair - 4:
                                    _w(v, S["Ex"], ex_at[idx])
                                    if idx >= 2 and sched[idx - 2][5]:
                                        _w(v, S["Pc"], b["Pc"] + idx - 1)
                                    i0 = 2 * (i - (npair - 4))
                                    v.tensor_tensor(
                                        pmS[:, pb * 1024:(pb + 1) * 1024],
                                        psbS[:, pb * 1024:(pb + 1) * 1024],
                                        msb[:, i0 * 512:(i0 + 2) * 512],
                                        ALU.mult).then_inc(S["Mk"].h, 1)
                                idx += 1
                            _w(v, S["Pc"], b["Pc"] + head_first_pair[ih] + npair)
                            _w(v, S["bc"], b["bc"] + ih)
                            v.reciprocal(linv[:, :], psL[:, :]).then_inc(S["rc"].h, 1)
                            qh = kv * 4 + q4
                            _w(v, S["lb"], b["lb"] + ih + 1)
                            if g > 0:
                                _w(v, S["womm"], b["womm"] + 32 * g)
                            v.tensor_tensor(attnT[:, qh * 512:(qh + 1) * 512],
                                            psPV[:, (ih % 2) * 512:(ih % 2 + 1) * 512],
                                            lbc[:, (ih % 2) * 512:(ih % 2 + 1) * 512],
                                            ALU.mult).then_inc(S["at"].h, 1)
            S["Mk"].n += 4 * 128
            S["rc"].n += 128
            S["at"].n += 128

    def phase_f_a(tb, hidden):
        b = {n: S[n].n for n in ("xnl", "fw", "lwl", "axe", "gmm", "umm",
                                 "sil", "hid")}
        with (
            nc.sbuf_tensor("xnt_tb", [128, DCH * 512], BF16) as xnt_tb,
            nc.sbuf_tensor("gwb", [128, 2 * DCH * 128], BF16) as gwb,
            nc.sbuf_tensor("uwb", [128, 2 * DCH * 128], BF16) as uwb,
            nc.sbuf_tensor("gbsb", [16, HCH * 128], BF16) as gbsb,
            nc.sbuf_tensor("ubsb", [16, HCH * 128], BF16) as ubsb,
            nc.sbuf_tensor("guasb", [128, DCH * 32], BF16) as guasb,
            nc.sbuf_tensor("axg", [16, 512], BF16) as axg,
            nc.sbuf_tensor("axu", [16, 512], BF16) as axu,
            nc.sbuf_tensor("sgsb", [128, 2 * 512], BF16) as sgsb,
            nc.psum_tensor("psG", [128, 2 * 512], F32) as psG,
            nc.psum_tensor("psU", [128, 2 * 512], F32) as psU,
            nc.psum_tensor("psAXg", [16, 512], F32) as psAXg,
            nc.psum_tensor("psAXu", [16, 512], F32) as psAXu,
            nc.Block() as blk,
        ):
            @blk.sync
            def _(sy):
                _w(sy, S["xnw"], S["xnw"].n)
                _w(sy, S["sw"], S["sw"].n)
                _w(sy, S["p0s"], S["p0s"].n)
                sy.dma_start(xnt_tb[:, :].rearrange("p (c t) -> p c t", c=DCH),
                             xn_scr[:, :, tb * 512:(tb + 1) * 512].rearrange(
                                 "c p t -> p c t")).then_inc(S["xnl"].h, 16)
            S["xnl"].n += 16

            @blk.gpsimd
            def _(g):
                _w(g, S["sw"], S["sw"].n)
                g.dma_start(gbsb[:, :].rearrange("p (h m) -> p h m", h=HCH), gb
                            ).then_inc(S["lwl"].h, 16)
                g.dma_start(ubsb[:, :].rearrange("p (h m) -> p h m", h=HCH), ub
                            ).then_inc(S["lwl"].h, 16)
                g.dma_start(guasb[:, :].rearrange("p (c m) -> p c m", c=DCH), gua
                            ).then_inc(S["lwl"].h, 16)
                _w(g, S["p0s"], S["p0s"].n)
                for h in range(HCH):
                    _w(g, S["gmm"], b["gmm"] + h)
                    _w(g, S["umm"], b["umm"] + h - 1)
                    g.dma_start(gwb[:, (h % 2) * DCH * 128:(h % 2 + 1) * DCH * 128],
                                gw_scr[h]).then_inc(S["fw"].h, 16)
                    g.dma_start(uwb[:, (h % 2) * DCH * 128:(h % 2 + 1) * DCH * 128],
                                uw_scr[h]).then_inc(S["fw"].h, 16)
            S["lwl"].n += 48
            S["fw"].n += 32 * HCH

            @blk.tensor
            def _(t):
                _w(t, S["xnl"], b["xnl"] + 16)
                _w(t, S["lwl"], b["lwl"] + 48)
                for c in range(DCH):
                    t.matmul(psAXg[:, :], guasb[:, c * 32:c * 32 + 16],
                             xnt_tb[:, c * 512:(c + 1) * 512],
                             start=(c == 0), stop=(c == DCH - 1))
                    mm = t.matmul(psAXu[:, :], guasb[:, c * 32 + 16:(c + 1) * 32],
                                  xnt_tb[:, c * 512:(c + 1) * 512],
                                  start=(c == 0), stop=(c == DCH - 1))
                mm.then_inc(S["gmm"].h, 1)
                for h in range(HCH):
                    _w(t, S["fw"], b["fw"] + 32 * (h + 1))
                    _w(t, S["sil"], b["sil"] + h - 1)
                    _w(t, S["hid"], b["hid"] + h - 1)
                    if h == 0:
                        _w(t, S["axe"], b["axe"] + 1)
                    wb_ = (h % 2) * DCH * 128
                    pb = (h % 2) * 512
                    for c in range(DCH):
                        t.matmul(psG[:, pb:pb + 512],
                                 gwb[:, wb_ + c * 128:wb_ + (c + 1) * 128],
                                 xnt_tb[:, c * 512:(c + 1) * 512],
                                 start=(c == 0), stop=False)
                    t.matmul(psG[:, pb:pb + 512], gbsb[:, h * 128:(h + 1) * 128],
                             axg[:, :], start=False, stop=True
                             ).then_inc(S["gmm"].h, 1)
                    for c in range(DCH):
                        t.matmul(psU[:, pb:pb + 512],
                                 uwb[:, wb_ + c * 128:wb_ + (c + 1) * 128],
                                 xnt_tb[:, c * 512:(c + 1) * 512],
                                 start=(c == 0), stop=False)
                    t.matmul(psU[:, pb:pb + 512], ubsb[:, h * 128:(h + 1) * 128],
                             axu[:, :], start=False, stop=True
                             ).then_inc(S["umm"].h, 1)
            S["gmm"].n += 1 + HCH
            S["umm"].n += HCH

            @blk.scalar
            def _(s):
                _w(s, S["gmm"], b["gmm"] + 1)
                s.copy(axg[:, :], psAXg[:, :])
                s.copy(axu[:, :], psAXu[:, :]).then_inc(S["axe"].h, 1)
                for h in range(HCH):
                    _w(s, S["gmm"], b["gmm"] + 1 + h + 1)
                    _w(s, S["hid"], b["hid"] + h - 1)
                    s.activation(sgsb[:, (h % 2) * 512:(h % 2 + 1) * 512],
                                 psG[:, (h % 2) * 512:(h % 2 + 1) * 512],
                                 AF.Silu).then_inc(S["sil"].h, 1)
            S["axe"].n += 1
            S["sil"].n += HCH

            @blk.vector
            def _(v):
                for h in range(HCH):
                    _w(v, S["sil"], b["sil"] + h + 1)
                    _w(v, S["umm"], b["umm"] + h + 1)
                    v.tensor_tensor(hidden[:, h * 512:(h + 1) * 512],
                                    sgsb[:, (h % 2) * 512:(h % 2 + 1) * 512],
                                    psU[:, (h % 2) * 512:(h % 2 + 1) * 512],
                                    ALU.mult).then_inc(S["hid"].h, 1)
            S["hid"].n += HCH

    def phase_f_b(tb, hidden, osb):
        b = {n: S[n].n for n in ("dwl", "dmm", "axd2", "gs", "sx", "cmb",
                                 "outw", "lwl", "axe", "hid")}
        with (
            nc.sbuf_tensor("dwb", [128, 2 * HCH * 128], BF16) as dwb,
            nc.sbuf_tensor("dasb", [128, HCH * 16], BF16) as dasb,
            nc.sbuf_tensor("dbsb", [16, DCH * 128], BF16) as dbsb,
            nc.sbuf_tensor("axdsb", [16, 512], BF16) as axdsb,
            nc.sbuf_tensor("gsb", [128, 2 * 512], F32) as gsb,
            nc.sbuf_tensor("ssb", [128, 2 * 512], F32) as ssb,
            nc.sbuf_tensor("xtb", [128, 2 * 512], F32) as xtb,
            nc.sbuf_tensor("tmb", [128, 512], F32) as tmb,
            nc.psum_tensor("psD", [128, 2 * 512], F32) as psD,
            nc.psum_tensor("psAXD", [16, 512], F32) as psAXD,
            nc.Block() as blk,
        ):
            @blk.gpsimd
            def _(g):
                g.dma_start(dasb[:, :].rearrange("p (h m) -> p h m", h=HCH), da
                            ).then_inc(S["lwl"].h, 16)
                g.dma_start(dbsb[:, :].rearrange("p (c m) -> p c m", c=DCH), db
                            ).then_inc(S["lwl"].h, 16)
                for dm in range(DCH):
                    _w(g, S["dmm"], b["dmm"] + dm - 1)
                    g.dma_start(dwb[:, (dm % 2) * HCH * 128:(dm % 2 + 1) * HCH * 128],
                                dw_scr[dm]).then_inc(S["dwl"].h, 16)
            S["lwl"].n += 32
            S["dwl"].n += 16 * DCH

            @blk.tensor
            def _(t):
                _w(t, S["lwl"], b["lwl"] + 32)
                _w(t, S["hid"], b["hid"])
                for h in range(HCH):
                    mm = t.matmul(psAXD[:, :], dasb[:, h * 16:(h + 1) * 16],
                                  hidden[:, h * 512:(h + 1) * 512],
                                  start=(h == 0), stop=(h == HCH - 1))
                mm.then_inc(S["axd2"].h, 1)
                for dm in range(DCH):
                    _w(t, S["dwl"], b["dwl"] + 16 * (dm + 1))
                    _w(t, S["gs"], b["gs"] + dm - 1)
                    if dm == 0:
                        _w(t, S["axe"], b["axe"] + 1)
                    wb_ = (dm % 2) * HCH * 128
                    pb = (dm % 2) * 512
                    for h in range(HCH):
                        t.matmul(psD[:, pb:pb + 512],
                                 dwb[:, wb_ + h * 128:wb_ + (h + 1) * 128],
                                 hidden[:, h * 512:(h + 1) * 512],
                                 start=(h == 0), stop=False)
                    t.matmul(psD[:, pb:pb + 512], dbsb[:, dm * 128:(dm + 1) * 128],
                             axdsb[:, :], start=False, stop=True
                             ).then_inc(S["dmm"].h, 1)
            S["axd2"].n += 1
            S["dmm"].n += DCH

            @blk.scalar
            def _(s):
                _w(s, S["axd2"], b["axd2"] + 1)
                s.copy(axdsb[:, :], psAXD[:, :]).then_inc(S["axe"].h, 1)
                for dm in range(DCH):
                    _w(s, S["dmm"], b["dmm"] + dm + 1)
                    _w(s, S["cmb"], b["cmb"] + dm - 1)
                    s.activation(gsb[:, (dm % 2) * 512:(dm % 2 + 1) * 512],
                                 psD[:, (dm % 2) * 512:(dm % 2 + 1) * 512],
                                 AF.Silu).then_inc(S["gs"].h, 1)
                    if dm == 0:
                        _w(s, S["sw"], S["sw"].n)
                        _w(s, S["xtw"], S["xtw"].n)
                    _w(s, S["cmb"], b["cmb"] + dm - 1)
                    s.dma_start(ssb[:, (dm % 2) * 512:(dm % 2 + 1) * 512],
                                s_scr[dm, :, tb * 512:(tb + 1) * 512]
                                ).then_inc(S["sx"].h, 16)
                    s.dma_start(xtb[:, (dm % 2) * 512:(dm % 2 + 1) * 512],
                                xqt[dm * 128:(dm + 1) * 128,
                                    tb * 512:(tb + 1) * 512]
                                ).then_inc(S["sx"].h, 16)
            S["axe"].n += 1
            S["gs"].n += DCH
            S["sx"].n += 32 * DCH

            @blk.vector
            def _(v):
                for dm in range(DCH):
                    _w(v, S["gs"], b["gs"] + dm + 1)
                    _w(v, S["sx"], b["sx"] + 32 * (dm + 1))
                    _w(v, S["outw"], b["outw"] + 16 * (dm - 1) if dm >= 2
                       else b["outw"])
                    bb = (dm % 2) * 512
                    v.tensor_tensor(tmb[:, :], ssb[:, bb:bb + 512],
                                    gsb[:, bb:bb + 512], ALU.mult)
                    v.tensor_tensor(osb[:, bb:bb + 512], tmb[:, :],
                                    xtb[:, bb:bb + 512], ALU.add
                                    ).then_inc(S["cmb"].h, 1)
            S["cmb"].n += DCH

            @blk.sync
            def _(sy):
                for dm in range(DCH):
                    _w(sy, S["cmb"], b["cmb"] + dm + 1)
                    sy.dma_start(out_t[dm * 128:(dm + 1) * 128,
                                       tb * 512:(tb + 1) * 512],
                                 osb[:, (dm % 2) * 512:(dm % 2 + 1) * 512]
                                 ).then_inc(S["outw"].h, 16)
            S["outw"].n += 16 * DCH

    nhalf = {"LN0": 1, "K0": 1, "Q": 0, "LNX": 0}.get(PHASES, 2)
    with nc.sbuf_tensor("xnt_half", [128, DCH * 2048], BF16) as xnt_half:
        for half in range(nhalf):
            with nc.named_scope(f"lnK{half}"):
                ln_block(x, half * 2048, 16, xnt_half, 2048)
            if PHASES != "LN0":
                with nc.named_scope(f"phK{half}"):
                    phase_k(half, xnt_half)
    if PHASES not in ("LN0", "K0", "K"):
        with nc.sbuf_tensor("xnt_own", [128, DCH * 1024], BF16) as xnt_own:
            for oh in range(1 if PHASES == "LNX" else 2):
                with nc.named_scope(f"lnQ{oh}"):
                    ln_block(xq, oh * 1024, 8, xnt_own, 1024)
                if PHASES != "LNX":
                    with nc.named_scope(f"phQ{oh}"):
                        phase_q(oh, xnt_own)
    if PHASES in ("A", "ALL"):
        with nc.named_scope("phA"):
            phase_a()
    if PHASES == "ALL":
        with (
            nc.sbuf_tensor("hidden", [128, HCH * 512], BF16) as hidden,
            nc.sbuf_tensor("osb", [128, 2 * 512], F32) as osb,
        ):
            for tb in range(NG):
                with nc.named_scope(f"phFa{tb}"):
                    phase_f_a(tb, hidden)
                with nc.named_scope(f"phFb{tb}"):
                    phase_f_b(tb, hidden, osb)
        with nc.Block() as blk:
            @blk.sync
            def _(sy):
                sy.wait_ge(S["outw"].h, S["outw"].n)
    else:
        with nc.Block() as blk:
            final = [(S["kvw"], S["kvw"].n), (S["qw"], S["qw"].n),
                     (S["xnw"], S["xnw"].n), (S["xtw"], S["xtw"].n)]
            if PHASES == "A":
                final.append((S["sw"], S["sw"].n))

            @blk.sync
            def _(sy):
                for sem, val in final:
                    sy.wait_ge(sem.h, val)
    return nc


def _perm():
    p = np.empty(128, np.int64)
    for m in range(64):
        p[m] = 2 * m
        p[m + 64] = 2 * m + 1
    return p


def _rope_tables():
    inv = (1.0 / (10000.0 ** (np.arange(0, 128, 2, dtype=np.float32) / np.float32(128.0)))
           ).astype(np.float32)
    pos = np.arange(SEQ, dtype=np.float32)
    ang = (pos[:, None] * inv[None, :]).astype(np.float32)
    cos = np.cos(ang).astype(np.float32).T
    sin = np.sin(ang).astype(np.float32).T
    cos_t = np.concatenate([cos, cos], axis=0)
    sin_t = np.concatenate([-sin, sin], axis=0)
    return np.ascontiguousarray(cos_t), np.ascontiguousarray(sin_t)


def _prep_shared(Wq, Wk, Wv, Wo, gate_w, up_w, down_w, ga, gbm, ua, ubm, dam, dbm):
    perm = _perm()
    pc = np.concatenate([h * 128 + perm for h in range(NH)])
    Wq_p = Wq[:, pc]
    pk = np.concatenate([h * 128 + perm for h in range(NKV)])
    Wk_p = Wk[:, pk]
    o = {}
    o["wq"] = np.ascontiguousarray(Wq_p.reshape(DCH, 128, NH, 128).transpose(2, 1, 0, 3))
    o["wk"] = np.ascontiguousarray(Wk_p.reshape(DCH, 128, NKV, 128).transpose(2, 1, 0, 3))
    o["wv"] = np.ascontiguousarray(Wv.reshape(DCH, 128, 4, 256).transpose(2, 1, 0, 3))
    o["wo"] = np.ascontiguousarray(Wo.reshape(DCH, 128, DCH, 128).transpose(2, 1, 0, 3))
    o["gw"] = np.ascontiguousarray(gate_w.reshape(HCH, 128, DCH, 128).transpose(0, 3, 2, 1))
    o["uw"] = np.ascontiguousarray(up_w.reshape(HCH, 128, DCH, 128).transpose(0, 3, 2, 1))
    o["dw"] = np.ascontiguousarray(down_w.reshape(DCH, 128, HCH, 128).transpose(0, 3, 2, 1))
    A = np.concatenate([ga, ua], axis=0)
    o["gua"] = np.ascontiguousarray(A.reshape(32, DCH, 128).transpose(2, 1, 0))
    o["gb"] = np.ascontiguousarray(gbm.reshape(HCH, 128, 16).transpose(2, 0, 1))
    o["ub"] = np.ascontiguousarray(ubm.reshape(HCH, 128, 16).transpose(2, 0, 1))
    o["da"] = np.ascontiguousarray(dam.reshape(16, HCH, 128).transpose(2, 1, 0))
    o["db"] = np.ascontiguousarray(dbm.reshape(DCH, 128, 16).transpose(2, 0, 1))
    o["idbf"] = np.eye(128, dtype=BF)
    o["idf32"] = np.eye(128, dtype=np.float32)
    return o


def _core_js(half):
    return [7, 5, 3, 1] if half == 0 else [6, 4, 2, 0]


def _masks_for(js):
    m = np.zeros((NG, 8, 128, GSZ), dtype=BF)
    for g in range(NG):
        j = js[g]
        for i in range(8):
            t = CNT[g] - 8 + i
            kp = t * 128 + np.arange(128)
            qp = j * 512 + np.arange(GSZ)
            m[g, i] = (kp[:, None] <= qp[None, :]).astype(BF)
    return m


def kernel(L0_attn_x, L0_attn_Wq, L0_attn_Wk, L0_attn_Wv, L0_attn_Wo,
           L0_ffn_gate_w, L0_ffn_up_w, L0_ffn_down_w,
           L0_ffn_gate_a, L0_ffn_gate_b, L0_ffn_up_a, L0_ffn_up_b,
           L0_ffn_down_a, L0_ffn_down_b):
    xf = np.asarray(L0_attn_x, dtype=np.float32)
    shared = _prep_shared(
        np.asarray(L0_attn_Wq, np.float32), np.asarray(L0_attn_Wk, np.float32),
        np.asarray(L0_attn_Wv, np.float32), np.asarray(L0_attn_Wo, np.float32),
        np.asarray(L0_ffn_gate_w, np.float32), np.asarray(L0_ffn_up_w, np.float32),
        np.asarray(L0_ffn_down_w, np.float32),
        np.asarray(L0_ffn_gate_a, np.float32), np.asarray(L0_ffn_gate_b, np.float32),
        np.asarray(L0_ffn_up_a, np.float32), np.asarray(L0_ffn_up_b, np.float32),
        np.asarray(L0_ffn_down_a, np.float32), np.asarray(L0_ffn_down_b, np.float32))
    cos_t, sin_t = _rope_tables()
    shared["kcos"] = cos_t
    shared["ksin"] = sin_t

    in_maps = []
    for c in range(8):
        bat, half = c // 2, c % 2
        js = _core_js(half)
        qpos = np.concatenate([np.arange(j * 512, (j + 1) * 512) for j in js])
        m = dict(shared)
        m["x"] = np.ascontiguousarray(xf[bat])
        m["xq"] = np.ascontiguousarray(xf[bat][qpos])
        m["xqt"] = np.ascontiguousarray(xf[bat][qpos].T)
        m["qcos"] = np.ascontiguousarray(cos_t[:, qpos])
        m["qsin"] = np.ascontiguousarray(sin_t[:, qpos])
        m["msk"] = _masks_for(js)
        in_maps.append(m)

    nc = build_nc()
    res = run_bass_kernel_spmd(nc, in_maps, core_ids=list(range(8)))

    out = np.empty((4, SEQ, D), dtype=np.float32)
    for c in range(8):
        bat, half = c // 2, c % 2
        js = _core_js(half)
        ot = np.ascontiguousarray(np.asarray(res.results[c]["out_t"]).T)
        for g in range(NG):
            j = js[g]
            out[bat, j * 512:(j + 1) * 512, :] = ot[g * 512:(g + 1) * 512, :]
    if DEBUG:
        kernel._dbg = res
    return out


# version 12 (final)
# speedup vs baseline: 1.0606x; 1.0606x over previous
import numpy as np
import ml_dtypes
from contextlib import ExitStack

import concourse.bass as bass
import concourse.mybir as mybir
from concourse.bass_utils import run_bass_kernel_spmd

F32 = mybir.dt.float32
BF16 = mybir.dt.bfloat16
AF = mybir.ActivationFunctionType
ALU = mybir.AluOpType
BF = ml_dtypes.bfloat16

D = 4096
DCH = 32
SEQ = 4096
HD = 128
NH = 32
NKV = 8
HID = 11008
HCH = 86
NG = 4
GSZ = 512
OWN = NG * GSZ
CNT = [32, 24, 16, 8]
ISCALE = 0.08838834764831845
EPS = 1e-5

DEBUG = False
PHASES = "ALL"


class SemCtr:
    __slots__ = ("h", "n")

    def __init__(self, h):
        self.h = h
        self.n = 0


def _w(eng, sem, val):
    if val > 0:
        eng.wait_ge(sem.h, val)


def build_nc():
    nc = bass.Bass("TRN2", target_bir_lowering=False)

    def din(name, shape, dtype=F32):
        return nc.dram_tensor(name, shape, dtype, kind="ExternalInput").ap()

    def dscr(name, shape, dtype):
        kind = "ExternalOutput" if DEBUG else "Internal"
        return nc.dram_tensor(name, shape, dtype, kind=kind).ap()

    x = din("x", [SEQ, D])
    xq = din("xq", [OWN, D])
    xqt = din("xqt", [D, OWN])
    qcos = din("qcos", [128, OWN]); qsin = din("qsin", [128, OWN])
    kcos = din("kcos", [128, SEQ]); ksin = din("ksin", [128, SEQ])
    msk = din("msk", [NG, 8, 128, GSZ], BF16)
    idbf = din("idbf", [128, 128], BF16)
    idf32 = din("idf32", [128, 128], F32)
    wq = din("wq", [NH, 128, DCH, 128])
    wk = din("wk", [NKV, 128, DCH, 128])
    wv = din("wv", [4, 128, DCH, 256])
    wo = din("wo", [DCH, 128, DCH, 128])
    gw = din("gw", [HCH, 128, DCH, 128])
    uw = din("uw", [HCH, 128, DCH, 128])
    dw = din("dw", [DCH, 128, HCH, 128])
    gua = din("gua", [128, DCH, 32])
    gb = din("gb", [16, HCH, 128])
    ub = din("ub", [16, HCH, 128])
    da = din("da", [128, HCH, 16])
    db = din("db", [16, DCH, 128])
    out_t = nc.dram_tensor("out_t", [D, OWN], F32, kind="ExternalOutput").ap()

    k_scr = dscr("k_scr", [NKV, 128, SEQ], BF16)
    v_scr = dscr("v_scr", [NKV, SEQ, HD], BF16)
    q_scr = dscr("q_scr", [NH, 128, OWN], BF16)
    xn_scr = dscr("xn_scr", [DCH, 128, OWN], BF16)
    s_scr = dscr("s_scr", [DCH, 128, OWN], F32)

    es = ExitStack()
    nc._kernel_es = es

    _uid = [0]
    _orig_sbuf = nc.sbuf_tensor
    _orig_psum = nc.psum_tensor

    def _usb(name, shape, dtype, **kw):
        _uid[0] += 1
        return _orig_sbuf(f"{name}_{_uid[0]}", shape, dtype, **kw)

    def _ups(name, shape, dtype, **kw):
        _uid[0] += 1
        return _orig_psum(f"{name}_{_uid[0]}", shape, dtype, **kw)

    nc.sbuf_tensor = _usb
    nc.psum_tensor = _ups

    S = {n: SemCtr(es.enter_context(nc.semaphore(n))) for n in [
        "ini", "xld", "ln", "rs", "xn", "tp", "ev", "xts", "xtw", "cs",
        "wK", "mmK", "ropS", "ropV", "kvw",
        "wQ", "mmQ", "qw", "xnw",
        "kvL", "qL", "mL", "Smm", "Ex", "Mk", "Pc", "rc", "bc", "lb", "at",
        "woL", "womm", "sev", "sw",
        "xnl", "fw", "lwl", "axe", "gmm", "umm", "sil", "hid",
        "dwl", "dmm", "axd2", "gs", "sx", "cmb", "outw",
        "fd0", "fd1", "fd2", "fg0", "fg1", "fg2", "fu",
    ]}

    id_bf = es.enter_context(nc.sbuf_tensor("id_bf", [128, 128], BF16))
    ones_bf = es.enter_context(nc.sbuf_tensor("ones_bf", [128, 1], BF16))
    onesr_f = es.enter_context(nc.sbuf_tensor("onesr_f", [1, 128], F32))

    with nc.Block() as blk:
        @blk.gpsimd
        def _(g):
            g.dma_start(id_bf[:, :], idbf).then_inc(S["ini"].h, 16)
            g.memset(ones_bf[:, :], 1.0)
            g.memset(onesr_f[:, :], 1.0)
            g.wait_ge(S["ini"].h, 16)
    S["ini"].n = 16

    def ln_block(src_ap, row0, nrt, xnt, tpc):
        b_xld = S["xld"].n; b_ln = S["ln"].n; b_rs = S["rs"].n; b_xn = S["xn"].n
        b_tp = S["tp"].n; b_ev = S["ev"].n
        with (
            nc.sbuf_tensor("xrow", [128, 2 * D], F32) as xrow,
            nc.sbuf_tensor("stats", [128, 48], F32) as stats,
            nc.sbuf_tensor("mv", [128, 2], F32) as mv,
            nc.sbuf_tensor("veps", [128, 1], F32) as veps,
            nc.sbuf_tensor("lnv", [128, 1], F32) as lnv,
            nc.sbuf_tensor("rstd", [128, 1], F32) as rstd,
            nc.sbuf_tensor("xnbf", [128, 2 * D], BF16) as xnbf,
            nc.sbuf_tensor("dmyv", [128, 512], F32) as dmyv,
            nc.psum_tensor("psT", [128, 2 * 1024], BF16) as psT,
            nc.Block() as blk,
        ):
            b_prev = [(S[n], S[n].n) for n in ("kvw", "qw", "xtw")]

            @blk.sync
            def _(sy):
                for sem, val in b_prev:
                    _w(sy, sem, val)
                for rt in range(nrt):
                    _w(sy, S["xn"], b_xn + rt - 1)
                    sy.dma_start(xrow[:, (rt % 2) * D:(rt % 2 + 1) * D],
                                 src_ap[row0 + rt * 128: row0 + (rt + 1) * 128, :]
                                 ).then_inc(S["xld"].h, 16)
            S["xld"].n += 16 * nrt

            @blk.vector
            def _(v):
                for rt in range(nrt):
                    b = (rt % 2) * D
                    _w(v, S["xld"], b_xld + 16 * (rt + 1))
                    for ch in range(8):
                        v.bn_stats(stats[:, 6 * ch:6 * ch + 6],
                                   xrow[:, b + 512 * ch: b + 512 * (ch + 1)])
                    v.bn_aggr(mv[:, :], stats[:, :])
                    v.tensor_tensor(dmyv[:, :], xrow[:, b:b + 512],
                                    xrow[:, b:b + 512], ALU.mult)
                    v.tensor_scalar_add(veps[:, :], mv[:, 1:2], EPS
                                        ).then_inc(S["ln"].h, 1)
                    _w(v, S["rs"], b_rs + rt + 1)
                    _w(v, S["tp"], b_tp + 4 * (rt - 1))
                    v.tensor_scalar(xnbf[:, b:b + D], xrow[:, b:b + D],
                                    mv[:, 0:1], rstd[:, :],
                                    ALU.subtract, ALU.mult).then_inc(S["xn"].h, 1)
            S["ln"].n += nrt
            S["xn"].n += nrt

            xnt_v = xnt[:, :].rearrange("p (c t) -> p c t", c=DCH)

            def emit_evac(s, rt):
                for grp in range(4):
                    u = rt * 4 + grp
                    c0 = 8 * grp
                    _w(s, S["tp"], b_tp + u + 1)
                    s.copy(xnt_v[:, c0:c0 + 8, rt * 128: rt * 128 + 128],
                           psT[:, (u % 2) * 1024:(u % 2 + 1) * 1024].rearrange(
                               "p (c t) -> p c t", c=8)).then_inc(S["ev"].h, 1)

            @blk.scalar
            def _(s):
                for rt in range(nrt):
                    _w(s, S["ln"], b_ln + rt + 1)
                    s.activation(lnv[:, :], veps[:, :], AF.Ln)
                    s.activation(rstd[:, :], lnv[:, :], AF.Exp,
                                 scale=-0.5).then_inc(S["rs"].h, 1)
                    if rt >= 1:
                        emit_evac(s, rt - 1)
                emit_evac(s, nrt - 1)
            S["rs"].n += nrt
            S["ev"].n += 4 * nrt

            @blk.tensor
            def _(t):
                _w(t, S["ini"], 16)
                for rt in range(nrt):
                    b = (rt % 2) * D
                    _w(t, S["xn"], b_xn + rt + 1)
                    for grp in range(4):
                        u = rt * 4 + grp
                        _w(t, S["ev"], b_ev + (u - 1))
                        for j in range(8):
                            c = 8 * grp + j
                            mmT = t.transpose(
                                psT[:, (u % 2) * 1024 + j * 128:
                                    (u % 2) * 1024 + (j + 1) * 128],
                                xnbf[:, b + c * 128:b + (c + 1) * 128], id_bf[:, :])
                        mmT.then_inc(S["tp"].h, 1)
            S["tp"].n += 4 * nrt

    def phase_k(half, xnt):
        hbase = half * 2048
        b = {n: S[n].n for n in ("wK", "mmK", "ropS", "ropV", "kvw", "cs")}
        xnt_v = xnt[:, :].rearrange("p (c t) -> p c t", c=DCH)
        with (
            nc.sbuf_tensor("ccos", [128, 2048], F32) as ccos,
            nc.sbuf_tensor("csin", [128, 2048], F32) as csin,
            nc.sbuf_tensor("wkb", [128, 2 * DCH * 128], BF16) as wkb,
            nc.sbuf_tensor("wvb", [128, DCH * 256], BF16) as wvb,
            nc.sbuf_tensor("krf", [128, 2 * 512], F32) as krf,
            nc.sbuf_tensor("ksf", [128, 2 * 512], F32) as ksf,
            nc.sbuf_tensor("rt1", [128, 512], F32) as rt1,
            nc.sbuf_tensor("rt2", [128, 512], F32) as rt2,
            nc.sbuf_tensor("kfin", [128, 2 * 512], BF16) as kfin,
            nc.sbuf_tensor("vstg", [128, 2 * 256], BF16) as vstg,
            nc.psum_tensor("psK", [128, 2 * 512], F32) as psK,
            nc.psum_tensor("psV", [128, 2 * 512], F32) as psV,
            nc.Block() as blk,
        ):
            @blk.sync
            def _(sy):
                sy.dma_start(ccos[:, :], kcos[:, hbase:hbase + 2048]
                             ).then_inc(S["cs"].h, 16)
                sy.dma_start(csin[:, :], ksin[:, hbase:hbase + 2048]
                             ).then_inc(S["cs"].h, 16)
                for kv in range(NKV):
                    for tg in range(4):
                        u = kv * 4 + tg
                        _w(sy, S["ropV"], b["ropV"] + u + 1)
                        sy.dma_start(
                            k_scr[kv, :, hbase + tg * 512: hbase + (tg + 1) * 512],
                            kfin[:, (u % 2) * 512:(u % 2 + 1) * 512]
                        ).then_inc(S["kvw"].h, 16)
                for hq in range(4):
                    for tt in range(16):
                        u2 = hq * 16 + tt
                        _w(sy, S["ropS"], b["ropS"] + 32 + u2 + 1)
                        tok0 = hbase + tt * 128
                        sy.dma_start(
                            v_scr[2 * hq:2 * hq + 2, tok0:tok0 + 128, :].rearrange(
                                "h p m -> p h m"),
                            vstg[:, (u2 % 2) * 256:(u2 % 2 + 1) * 256].rearrange(
                                "p (h m) -> p h m", h=2)
                        ).then_inc(S["kvw"].h, 16)
            S["cs"].n += 32
            S["kvw"].n += 16 * (32 + 64)

            @blk.gpsimd
            def _(g):
                for kv in range(NKV):
                    _w(g, S["mmK"], b["mmK"] + 4 * (kv - 1))
                    g.dma_start(wkb[:, :].rearrange("p (z c m) -> p z c m", z=2, c=DCH)
                                [:, kv % 2], wk[kv]).then_inc(S["wK"].h, 16)
                for hq in range(4):
                    _w(g, S["mmK"], b["mmK"] + 32 + 16 * hq)
                    g.dma_start(wvb[:, :].rearrange("p (c m) -> p c m", c=DCH),
                                wv[hq]).then_inc(S["wK"].h, 16)
            S["wK"].n += 16 * 12

            @blk.tensor
            def _(t):
                for kv in range(NKV):
                    _w(t, S["wK"], b["wK"] + 16 * (kv + 1))
                    for tg in range(4):
                        u = kv * 4 + tg
                        _w(t, S["ropS"], b["ropS"] + (u - 1))
                        for c in range(DCH):
                            mm = t.matmul(
                                psK[:, (u % 2) * 512:(u % 2 + 1) * 512],
                                wkb[:, (kv % 2) * DCH * 128 + c * 128:
                                    (kv % 2) * DCH * 128 + (c + 1) * 128],
                                xnt_v[:, c, tg * 512:(tg + 1) * 512],
                                start=(c == 0), stop=(c == DCH - 1))
                        mm.then_inc(S["mmK"].h, 1)
                for hq in range(4):
                    _w(t, S["wK"], b["wK"] + 16 * (8 + hq + 1))
                    for tt in range(16):
                        u2 = hq * 16 + tt
                        _w(t, S["ropS"], b["ropS"] + 32 + (u2 - 1))
                        for c in range(DCH):
                            mm = t.matmul(
                                psV[:, (u2 % 2) * 512:(u2 % 2) * 512 + 256],
                                xnt_v[:, c, tt * 128:(tt + 1) * 128],
                                wvb[:, c * 256:(c + 1) * 256],
                                start=(c == 0), stop=(c == DCH - 1))
                        mm.then_inc(S["mmK"].h, 1)
            S["mmK"].n += 32 + 64

            @blk.scalar
            def _(s):
                for kv in range(NKV):
                    for tg in range(4):
                        u = kv * 4 + tg
                        sb = (u % 2) * 512
                        _w(s, S["mmK"], b["mmK"] + u + 1)
                        _w(s, S["ropV"], b["ropV"] + (u - 1))
                        s.copy(krf[:, sb:sb + 512], psK[:, sb:sb + 512])
                        s.copy(ksf[0:64, sb:sb + 512], psK[64:128, sb:sb + 512])
                        s.copy(ksf[64:128, sb:sb + 512], psK[0:64, sb:sb + 512]
                               ).then_inc(S["ropS"].h, 1)
                for hq in range(4):
                    for tt in range(16):
                        u2 = hq * 16 + tt
                        sb = (u2 % 2) * 256
                        _w(s, S["mmK"], b["mmK"] + 32 + u2 + 1)
                        _w(s, S["kvw"], b["kvw"] + 16 * (32 + u2 - 1))
                        s.copy(vstg[:, sb:sb + 256],
                               psV[:, (u2 % 2) * 512:(u2 % 2) * 512 + 256]
                               ).then_inc(S["ropS"].h, 1)
            S["ropS"].n += 32 + 64

            @blk.vector
            def _(v):
                _w(v, S["cs"], b["cs"] + 32)
                for kv in range(NKV):
                    for tg in range(4):
                        u = kv * 4 + tg
                        sb = (u % 2) * 512
                        _w(v, S["ropS"], b["ropS"] + u + 1)
                        _w(v, S["kvw"], b["kvw"] + 16 * (u - 1))
                        cs_ = slice(tg * 512, (tg + 1) * 512)
                        v.tensor_tensor(rt1[:, :], ksf[:, sb:sb + 512],
                                        csin[:, cs_], ALU.mult)
                        v.tensor_tensor(rt2[:, :], krf[:, sb:sb + 512],
                                        ccos[:, cs_], ALU.mult)
                        v.tensor_tensor(kfin[:, sb:sb + 512], rt1[:, :], rt2[:, :],
                                        ALU.add).then_inc(S["ropV"].h, 1)
            S["ropV"].n += 32

    def phase_q(oh, xnt_own):
        b = {n: S[n].n for n in ("wQ", "mmQ", "ropS", "ropV", "qw", "cs", "xnw")}
        xnt_v = xnt_own[:, :].rearrange("p (c t) -> p c t", c=DCH)
        b_prevq = [(S[n], S[n].n) for n in ("xtw", "xnw", "qw")]
        with (
            nc.sbuf_tensor("qcosb", [128, 1024], F32) as qcosb,
            nc.sbuf_tensor("qsinb", [128, 1024], F32) as qsinb,
            nc.sbuf_tensor("wqb", [128, 2 * DCH * 128], BF16) as wqb,
            nc.sbuf_tensor("qrf", [128, 2 * 512], F32) as qrf,
            nc.sbuf_tensor("qsf", [128, 2 * 512], F32) as qsf,
            nc.sbuf_tensor("qt1", [128, 512], F32) as qt1,
            nc.sbuf_tensor("qt2", [128, 512], F32) as qt2,
            nc.sbuf_tensor("qfin", [128, 2 * 512], BF16) as qfin,
            nc.psum_tensor("psQ", [128, 2 * 512], F32) as psQ,
            nc.Block() as blk,
        ):
            @blk.sync
            def _(sy):
                for sem, val in b_prevq:
                    _w(sy, sem, val)
                sy.dma_start(qcosb[:, :], qcos[:, oh * 1024:(oh + 1) * 1024]
                             ).then_inc(S["cs"].h, 16)
                sy.dma_start(qsinb[:, :], qsin[:, oh * 1024:(oh + 1) * 1024]
                             ).then_inc(S["cs"].h, 16)
                sy.dma_start(
                    xn_scr[:, :, oh * 1024:(oh + 1) * 1024].rearrange("c p t -> p c t"),
                    xnt_own[:, :].rearrange("p (c t) -> p c t", c=DCH)
                ).then_inc(S["xnw"].h, 16)
                for qh in range(NH):
                    for lg in range(2):
                        u = qh * 2 + lg
                        _w(sy, S["ropV"], b["ropV"] + u + 1)
                        g_ = 2 * oh + lg
                        sy.dma_start(
                            q_scr[qh, :, g_ * 512:(g_ + 1) * 512],
                            qfin[:, (u % 2) * 512:(u % 2 + 1) * 512]
                        ).then_inc(S["qw"].h, 16)
            S["cs"].n += 32
            S["xnw"].n += 16
            S["qw"].n += 16 * 64

            @blk.gpsimd
            def _(g):
                for sem, val in b_prevq:
                    _w(g, sem, val)
                for qh in range(NH):
                    _w(g, S["mmQ"], b["mmQ"] + 2 * (qh - 1))
                    g.dma_start(wqb[:, :].rearrange("p (z c m) -> p z c m", z=2, c=DCH)
                                [:, qh % 2], wq[qh]).then_inc(S["wQ"].h, 16)
            S["wQ"].n += 16 * NH

            @blk.tensor
            def _(t):
                for qh in range(NH):
                    _w(t, S["wQ"], b["wQ"] + 16 * (qh + 1))
                    for lg in range(2):
                        u = qh * 2 + lg
                        _w(t, S["ropS"], b["ropS"] + (u - 1))
                        for c in range(DCH):
                            mm = t.matmul(
                                psQ[:, (u % 2) * 512:(u % 2 + 1) * 512],
                                wqb[:, (qh % 2) * DCH * 128 + c * 128:
                                    (qh % 2) * DCH * 128 + (c + 1) * 128],
                                xnt_v[:, c, lg * 512:(lg + 1) * 512],
                                start=(c == 0), stop=(c == DCH - 1))
                        mm.then_inc(S["mmQ"].h, 1)
            S["mmQ"].n += 64

            @blk.scalar
            def _(s):
                for qh in range(NH):
                    for lg in range(2):
                        u = qh * 2 + lg
                        sb = (u % 2) * 512
                        _w(s, S["mmQ"], b["mmQ"] + u + 1)
                        _w(s, S["ropV"], b["ropV"] + (u - 1))
                        s.copy(qrf[:, sb:sb + 512], psQ[:, sb:sb + 512])
                        s.copy(qsf[0:64, sb:sb + 512], psQ[64:128, sb:sb + 512])
                        s.copy(qsf[64:128, sb:sb + 512], psQ[0:64, sb:sb + 512]
                               ).then_inc(S["ropS"].h, 1)
            S["ropS"].n += 64

            @blk.vector
            def _(v):
                _w(v, S["cs"], b["cs"] + 32)
                for qh in range(NH):
                    for lg in range(2):
                        u = qh * 2 + lg
                        sb = (u % 2) * 512
                        _w(v, S["ropS"], b["ropS"] + u + 1)
                        _w(v, S["qw"], b["qw"] + 16 * (u - 1))
                        cs_ = slice(lg * 512, (lg + 1) * 512)
                        v.tensor_tensor(qt1[:, :], qsf[:, sb:sb + 512],
                                        qsinb[:, cs_], ALU.mult)
                        v.tensor_tensor(qt2[:, :], qrf[:, sb:sb + 512],
                                        qcosb[:, cs_], ALU.mult)
                        v.tensor_tensor(qfin[:, sb:sb + 512], qt1[:, :], qt2[:, :],
                                        ALU.add).then_inc(S["ropV"].h, 1)
            S["ropV"].n += 64

    def phase_a():
        b = {n: S[n].n for n in (
            "kvL", "qL", "mL", "Smm", "Ex", "Mk", "Pc", "rc", "bc", "lb",
            "at", "woL", "womm", "sev", "sw", "kvw", "qw", "xnw", "xtw")}

        sched = []
        for g in range(NG):
            npair = CNT[g] // 2
            for kv in range(NKV):
                for q4 in range(4):
                    ih = (g * 8 + kv) * 4 + q4
                    for i in range(npair):
                        sched.append((g, kv, q4, ih, i, i >= npair - 4))
        ex_at = {}
        mk_at = {}
        ne = b["Ex"]; nm = b["Mk"]
        for idx, rec in enumerate(sched):
            ne += 1
            ex_at[idx] = ne
            if rec[5]:
                nm += 1
                mk_at[idx] = nm
        head_first_pair = {}
        cacc = 0
        for g in range(NG):
            npair = CNT[g] // 2
            for kv in range(NKV):
                for q4 in range(4):
                    ih = (g * 8 + kv) * 4 + q4
                    head_first_pair[ih] = cacc
                    cacc += npair
        NPAIR = cacc

        def pc_through_head(ih):
            if ih < 0:
                return 0
            return head_first_pair[ih] + CNT[ih // 32] // 2

        def pc_through_kv(ikv):
            if ikv < 0:
                return 0
            return pc_through_head(ikv * 4 + 3)

        mk_cum = [0]
        for g in range(NG):
            mk_cum.append(mk_cum[-1] + 4 * 32)

        with (
            nc.sbuf_tensor("ksb", [128, 2 * 4096], BF16) as ksb,
            nc.sbuf_tensor("vsb", [128, 2 * 4096], BF16) as vsb,
            nc.sbuf_tensor("qsb", [128, 2 * 512], BF16) as qsb,
            nc.sbuf_tensor("psbS", [128, 3 * 1024], BF16) as psbS,
            nc.sbuf_tensor("pmS", [128, 3 * 1024], BF16) as pmS,
            nc.sbuf_tensor("msb", [128, 8 * 512], BF16) as msb,
            nc.sbuf_tensor("attnT", [128, NH * 512], BF16) as attnT,
            nc.sbuf_tensor("linv", [1, 512], F32) as linv,
            nc.sbuf_tensor("lbc", [128, 2 * 512], F32) as lbc,
            nc.sbuf_tensor("wob", [128, 2 * DCH * 128], BF16) as wob,
            nc.sbuf_tensor("sst", [128, 2 * 512], F32) as sst,
            nc.psum_tensor("psS", [128, 3 * 1024], F32) as psS,
            nc.psum_tensor("psPV", [128, 512], F32) as psPV,
            nc.psum_tensor("psLB", [128, 512], F32) as psLB,
            nc.Block() as blk,
        ):
            @blk.sync
            def _(sy):
                _w(sy, S["kvw"], b["kvw"])
                _w(sy, S["qw"], b["qw"])
                _w(sy, S["xnw"], b["xnw"])
                _w(sy, S["xtw"], b["xtw"])
                for g in range(NG):
                    cnt = CNT[g]
                    _w(sy, S["Mk"], b["Mk"] + mk_cum[g])
                    sy.dma_start(msb[:, :].rearrange("p (i f) -> p i f", i=8),
                                 msk[g].rearrange("i p f -> p i f")
                                 ).then_inc(S["mL"].h, 16)
                    for kv in range(NKV):
                        ikv = g * 8 + kv
                        _w(sy, S["Pc"], b["Pc"] + pc_through_kv(ikv - 2))
                        sy.dma_start(ksb[:, (ikv % 2) * 4096:
                                         (ikv % 2) * 4096 + cnt * 128],
                                     k_scr[kv, :, 0:cnt * 128]).then_inc(S["kvL"].h, 16)
                        sy.dma_start(vsb[:, (ikv % 2) * 4096:
                                         (ikv % 2) * 4096 + cnt * 128]
                                     .rearrange("p (t m) -> p t m", m=128),
                                     v_scr[kv, 0:cnt * 128, :].rearrange(
                                         "(t p) m -> p t m", p=128)
                                     ).then_inc(S["kvL"].h, 16)
                        for q4 in range(4):
                            ih = ikv * 4 + q4
                            qh = kv * 4 + q4
                            _w(sy, S["Smm"], b["Smm"] + head_first_pair.get(ih - 1, 0))
                            sy.dma_start(qsb[:, (ih % 2) * 512:(ih % 2 + 1) * 512],
                                         q_scr[qh, :, g * 512:(g + 1) * 512]
                                         ).then_inc(S["qL"].h, 16)
            S["mL"].n += 16 * NG
            S["kvL"].n += 32 * 32
            S["qL"].n += 16 * 128
            S["sw"].n += 16 * 128

            @blk.gpsimd
            def _(g_eng):
                for n in ("kvw", "qw", "xnw", "xtw"):
                    _w(g_eng, S[n], b[n])
                for g in range(NG):
                    for dm in range(DCH):
                        iw = g * 32 + dm
                        _w(g_eng, S["womm"], b["womm"] + iw - 1)
                        g_eng.dma_start(
                            wob[:, :].rearrange("p (z c m) -> p z c m", z=2, c=DCH)
                            [:, iw % 2], wo[dm]).then_inc(S["woL"].h, 16)
            S["woL"].n += 16 * 128

            @blk.tensor
            def _(t):
                def lpv(g, ih, kb, npair, i, pb, idx_):
                    masked = (i >= npair - 4)
                    if masked:
                        _w(t, S["Mk"], mk_at[idx_])
                    else:
                        _w(t, S["Ex"], ex_at[idx_])
                    if i == 0:
                        _w(t, S["rc"], b["rc"] + ih)
                        _w(t, S["lb"], b["lb"] + ih)
                        _w(t, S["at"], b["at"] + ih)
                    src = pmS if masked else psbS
                    mm = None
                    for hf in range(2):
                        tt = 2 * i + hf
                        colsrc = src[:, pb * 1024 + hf * 512:
                                     pb * 1024 + (hf + 1) * 512]
                        t.matmul(psLB[0:1, :], ones_bf[:, :], colsrc,
                                 start=(i == 0 and hf == 0), stop=False)
                        mm = t.matmul(psPV[:, :],
                                      vsb[:, kb + tt * 128:kb + (tt + 1) * 128],
                                      colsrc,
                                      start=(i == 0 and hf == 0),
                                      stop=(i == npair - 1 and hf == 1))
                    mm.then_inc(S["Pc"].h, 1)

                idx = 0
                for g in range(NG):
                    cnt = CNT[g]
                    npair = cnt // 2
                    if g > 0:
                        _w(t, S["womm"], b["womm"] + 32 * g)
                        _w(t, S["sev"], b["sev"] + 32 * g)
                    for kv in range(NKV):
                        ikv = g * 8 + kv
                        _w(t, S["kvL"], b["kvL"] + 32 * (ikv + 1))
                        kb = (ikv % 2) * 4096
                        for q4 in range(4):
                            ih = ikv * 4 + q4
                            qb = (ih % 2) * 512
                            _w(t, S["qL"], b["qL"] + 16 * (ih + 1))
                            pend = []
                            for i in range(npair):
                                pb = idx % 3
                                _w(t, S["Ex"], ex_at.get(idx - 3, 0))
                                if idx >= 3 and sched[idx - 3][5]:
                                    _w(t, S["Mk"], mk_at[idx - 3])
                                t.matmul(psS[:, pb * 1024:pb * 1024 + 512],
                                         ksb[:, kb + (2 * i) * 128:
                                             kb + (2 * i + 1) * 128],
                                         qsb[:, qb:qb + 512], start=True, stop=True)
                                t.matmul(psS[:, pb * 1024 + 512:pb * 1024 + 1024],
                                         ksb[:, kb + (2 * i + 1) * 128:
                                             kb + (2 * i + 2) * 128],
                                         qsb[:, qb:qb + 512], start=True,
                                         stop=True).then_inc(S["Smm"].h, 1)
                                pend.append((i, pb, idx))
                                idx += 1
                                if len(pend) >= 3:
                                    lpv(g, ih, kb, npair, *pend.pop(0))
                            while pend:
                                lpv(g, ih, kb, npair, *pend.pop(0))
                            _w(t, S["rc"], b["rc"] + ih + 1)
                            _w(t, S["lb"], b["lb"] + ih)
                            t.matmul(psLB[:, :], onesr_f[:, :], linv[:, :],
                                     start=True, stop=True).then_inc(S["bc"].h, 1)
                    _w(t, S["at"], b["at"] + 32 * (g + 1))
                    for dm in range(DCH):
                        iw = g * 32 + dm
                        _w(t, S["woL"], b["woL"] + 16 * (iw + 1))
                        _w(t, S["sev"], b["sev"] + iw - 1)
                        for c in range(DCH):
                            mm = t.matmul(
                                psS[:, (iw % 2) * 2048:(iw % 2) * 2048 + 512],
                                wob[:, (iw % 2) * DCH * 128 + c * 128:
                                    (iw % 2) * DCH * 128 + (c + 1) * 128],
                                attnT[:, c * 512:(c + 1) * 512],
                                start=(c == 0), stop=(c == DCH - 1))
                        mm.then_inc(S["womm"].h, 1)
            S["Smm"].n += NPAIR
            S["Pc"].n += NPAIR
            S["bc"].n += 128
            S["womm"].n += 128

            @blk.scalar
            def _(s):
                idx = 0
                for g in range(NG):
                    npair = CNT[g] // 2
                    for kv in range(NKV):
                        for q4 in range(4):
                            ih = (g * 8 + kv) * 4 + q4
                            for i in range(npair):
                                pb = idx % 3
                                _w(s, S["Smm"], b["Smm"] + idx + 1)
                                if idx >= 3:
                                    if sched[idx - 3][5]:
                                        _w(s, S["Mk"], mk_at[idx - 3])
                                    else:
                                        _w(s, S["Pc"], b["Pc"] + idx - 2)
                                s.activation(psbS[:, pb * 1024:(pb + 1) * 1024],
                                             psS[:, pb * 1024:(pb + 1) * 1024],
                                             AF.Exp, scale=ISCALE
                                             ).then_inc(S["Ex"].h, 1)
                                idx += 1
                            _w(s, S["bc"], b["bc"] + ih + 1)
                            _w(s, S["at"], b["at"] + ih - 1)
                            s.copy(lbc[:, (ih % 2) * 512:(ih % 2 + 1) * 512],
                                   psLB[:, :]).then_inc(S["lb"].h, 1)
                    for dm in range(DCH):
                        iw = g * 32 + dm
                        _w(s, S["womm"], b["womm"] + iw + 1)
                        _w(s, S["sw"], b["sw"] + 16 * (iw - 1))
                        s.copy(sst[:, (iw % 2) * 512:(iw % 2 + 1) * 512],
                               psS[:, (iw % 2) * 2048:(iw % 2) * 2048 + 512]
                               ).then_inc(S["sev"].h, 1)
                        s.dma_start(s_scr[dm, :, g * 512:(g + 1) * 512],
                                    sst[:, (iw % 2) * 512:(iw % 2 + 1) * 512]
                                    ).then_inc(S["sw"].h, 16)
            S["Ex"].n += NPAIR
            S["lb"].n += 128
            S["sev"].n += 128

            @blk.vector
            def _(v):
                idx = 0
                for g in range(NG):
                    npair = CNT[g] // 2
                    _w(v, S["mL"], b["mL"] + 16 * (g + 1))
                    for kv in range(NKV):
                        for q4 in range(4):
                            ih = (g * 8 + kv) * 4 + q4
                            for i in range(npair):
                                pb = idx % 3
                                if i >= npair - 4:
                                    _w(v, S["Ex"], ex_at[idx])
                                    if idx >= 3 and sched[idx - 3][5]:
                                        _w(v, S["Pc"], b["Pc"] + idx - 2)
                                    i0 = 2 * (i - (npair - 4))
                                    v.tensor_tensor(
                                        pmS[:, pb * 1024:(pb + 1) * 1024],
                                        psbS[:, pb * 1024:(pb + 1) * 1024],
                                        msb[:, i0 * 512:(i0 + 2) * 512],
                                        ALU.mult).then_inc(S["Mk"].h, 1)
                                idx += 1
                            _w(v, S["Pc"], b["Pc"] + head_first_pair[ih] + npair)
                            _w(v, S["bc"], b["bc"] + ih)
                            v.reciprocal(linv[:, :], psLB[0:1, :]).then_inc(S["rc"].h, 1)
                            qh = kv * 4 + q4
                            _w(v, S["lb"], b["lb"] + ih + 1)
                            if g > 0:
                                _w(v, S["womm"], b["womm"] + 32 * g)
                            v.tensor_tensor(attnT[:, qh * 512:(qh + 1) * 512],
                                            psPV[:, :],
                                            lbc[:, (ih % 2) * 512:(ih % 2 + 1) * 512],
                                            ALU.mult).then_inc(S["at"].h, 1)
            S["Mk"].n += 4 * 128
            S["rc"].n += 128
            S["at"].n += 128

    def phase_f_a(tb, hidden):
        b = {n: S[n].n for n in ("xnl", "fw", "lwl", "axe", "gmm", "umm",
                                 "sil", "hid", "fg0", "fg1", "fg2", "fu")}
        with (
            nc.sbuf_tensor("xnt_tb", [128, DCH * 512], BF16) as xnt_tb,
            nc.sbuf_tensor("gwb", [128, 3 * DCH * 128], BF16) as gwb,
            nc.sbuf_tensor("uwb", [128, 2 * DCH * 128], BF16) as uwb,
            nc.sbuf_tensor("gbsb", [16, HCH * 128], BF16) as gbsb,
            nc.sbuf_tensor("ubsb", [16, HCH * 128], BF16) as ubsb,
            nc.sbuf_tensor("axg", [16, 512], BF16) as axg,
            nc.sbuf_tensor("axu", [16, 512], BF16) as axu,
            nc.psum_tensor("psG", [128, 2 * 512], F32) as psG,
            nc.psum_tensor("psU", [128, 2 * 512], F32) as psU,
            nc.psum_tensor("psAXg", [16, 512], F32) as psAXg,
            nc.psum_tensor("psAXu", [16, 512], F32) as psAXu,
            nc.Block() as blk,
        ):
            @blk.sync
            def _(sy):
                _w(sy, S["xnw"], S["xnw"].n)
                _w(sy, S["sw"], S["sw"].n)
                _w(sy, S["outw"], S["outw"].n)
                sy.dma_start(xnt_tb[:, :].rearrange("p (c t) -> p c t", c=DCH),
                             xn_scr[:, :, tb * 512:(tb + 1) * 512].rearrange(
                                 "c p t -> p c t")).then_inc(S["xnl"].h, 16)
            S["xnl"].n += 16

            @blk.gpsimd
            def _(g):
                _w(g, S["sw"], S["sw"].n)
                g.dma_start(gbsb[:, :].rearrange("p (h m) -> p h m", h=HCH), gb
                            ).then_inc(S["lwl"].h, 16)
                g.dma_start(ubsb[:, :].rearrange("p (h m) -> p h m", h=HCH), ub
                            ).then_inc(S["lwl"].h, 16)
                g.dma_start(gwb[:, 2 * DCH * 128:2 * DCH * 128 + DCH * 32]
                            .rearrange("p (c m) -> p c m", c=DCH), gua
                            ).then_inc(S["lwl"].h, 16)
                for h in range(HCH):
                    _w(g, S["gmm"], b["gmm"] + h - 1)
                    _w(g, S["umm"], b["umm"] + h - 1)
                    fg = S[f"fg{h % 3}"]
                    g.dma_start(gwb[:, (h % 3) * DCH * 128:(h % 3 + 1) * DCH * 128],
                                gw[h].rearrange("p c m -> p (c m)")
                                ).then_inc(fg.h, 16)
                    fg.n += 16
                    g.dma_start(uwb[:, (h % 2) * DCH * 128:(h % 2 + 1) * DCH * 128],
                                uw[h].rearrange("p c m -> p (c m)")
                                ).then_inc(S["fu"].h, 16)
                    S["fu"].n += 16
            S["lwl"].n += 48

            @blk.tensor
            def _(t):
                _w(t, S["xnl"], b["xnl"] + 16)
                _w(t, S["lwl"], b["lwl"] + 48)
                gb2 = 2 * DCH * 128
                for c in range(DCH):
                    t.matmul(psAXg[:, :], gwb[:, gb2 + c * 32:gb2 + c * 32 + 16],
                             xnt_tb[:, c * 512:(c + 1) * 512],
                             start=(c == 0), stop=(c == DCH - 1))
                    mm = t.matmul(psAXu[:, :], gwb[:, gb2 + c * 32 + 16:gb2 + (c + 1) * 32],
                                  xnt_tb[:, c * 512:(c + 1) * 512],
                                  start=(c == 0), stop=(c == DCH - 1))
                mm.then_inc(S["gmm"].h, 1)
                for h in range(HCH):
                    _w(t, S[f"fg{h % 3}"], b[f"fg{h % 3}"] + 16 * (h // 3 + 1))
                    _w(t, S["fu"], b["fu"] + 16 * (h + 1))
                    _w(t, S["sil"], b["sil"] + h - 1)
                    _w(t, S["hid"], b["hid"] + h - 1)
                    if h == 0:
                        _w(t, S["axe"], b["axe"] + 1)
                    wb_ = (h % 3) * DCH * 128
                    ub_ = (h % 2) * DCH * 128
                    pb = (h % 2) * 512
                    for c in range(DCH):
                        t.matmul(psG[:, pb:pb + 512],
                                 gwb[:, wb_ + c * 128:wb_ + (c + 1) * 128],
                                 xnt_tb[:, c * 512:(c + 1) * 512],
                                 start=(c == 0), stop=False)
                    t.matmul(psG[:, pb:pb + 512], gbsb[:, h * 128:(h + 1) * 128],
                             axg[:, :], start=False, stop=True
                             ).then_inc(S["gmm"].h, 1)
                    for c in range(DCH):
                        t.matmul(psU[:, pb:pb + 512],
                                 uwb[:, ub_ + c * 128:ub_ + (c + 1) * 128],
                                 xnt_tb[:, c * 512:(c + 1) * 512],
                                 start=(c == 0), stop=False)
                    t.matmul(psU[:, pb:pb + 512], ubsb[:, h * 128:(h + 1) * 128],
                             axu[:, :], start=False, stop=True
                             ).then_inc(S["umm"].h, 1)
            S["gmm"].n += 1 + HCH
            S["umm"].n += HCH

            @blk.scalar
            def _(s):
                _w(s, S["gmm"], b["gmm"] + 1)
                s.copy(axg[:, :], psAXg[:, :])
                s.copy(axu[:, :], psAXu[:, :]).then_inc(S["axe"].h, 1)
                for h in range(HCH):
                    _w(s, S["gmm"], b["gmm"] + 1 + h + 1)
                    s.activation(hidden[:, h * 512:(h + 1) * 512],
                                 psG[:, (h % 2) * 512:(h % 2 + 1) * 512],
                                 AF.Silu).then_inc(S["sil"].h, 1)
            S["axe"].n += 1
            S["sil"].n += HCH

            @blk.vector
            def _(v):
                for h in range(HCH):
                    _w(v, S["sil"], b["sil"] + h + 1)
                    _w(v, S["umm"], b["umm"] + h + 1)
                    v.tensor_tensor(hidden[:, h * 512:(h + 1) * 512],
                                    hidden[:, h * 512:(h + 1) * 512],
                                    psU[:, (h % 2) * 512:(h % 2 + 1) * 512],
                                    ALU.mult).then_inc(S["hid"].h, 1)
            S["hid"].n += HCH

    def phase_f_b(tb, hidden):
        b = {n: S[n].n for n in ("dwl", "dmm", "axd2", "gs", "sx", "cmb",
                                 "outw", "lwl", "axe", "hid",
                                 "fd0", "fd1", "fd2")}
        with (
            nc.sbuf_tensor("dwb", [128, 3 * HCH * 128], BF16) as dwb,
            nc.sbuf_tensor("dasb", [128, HCH * 16], BF16) as dasb,
            nc.sbuf_tensor("dbsb", [16, DCH * 128], BF16) as dbsb,
            nc.sbuf_tensor("axdsb", [16, 512], BF16) as axdsb,
            nc.sbuf_tensor("gsb", [128, 2 * 512], F32) as gsb,
            nc.sbuf_tensor("ssb", [128, 2 * 512], F32) as ssb,
            nc.sbuf_tensor("xtb", [128, 2 * 512], F32) as xtb,
            nc.sbuf_tensor("tmb", [128, 512], F32) as tmb,
            nc.sbuf_tensor("osb", [128, 2 * 512], F32) as osb,
            nc.psum_tensor("psD", [128, 2 * 512], F32) as psD,
            nc.psum_tensor("psAXD", [16, 512], F32) as psAXD,
            nc.Block() as blk,
        ):
            @blk.gpsimd
            def _(g):
                g.dma_start(dasb[:, :].rearrange("p (h m) -> p h m", h=HCH), da
                            ).then_inc(S["lwl"].h, 16)
                g.dma_start(dbsb[:, :].rearrange("p (c m) -> p c m", c=DCH), db
                            ).then_inc(S["lwl"].h, 16)
                for dm in range(DCH):
                    _w(g, S["dmm"], b["dmm"] + dm - 2)
                    fd = S[f"fd{dm % 3}"]
                    g.dma_start(dwb[:, (dm % 3) * HCH * 128:(dm % 3 + 1) * HCH * 128],
                                dw[dm].rearrange("p c m -> p (c m)")
                                ).then_inc(fd.h, 16)
                    fd.n += 16
            S["lwl"].n += 32

            @blk.tensor
            def _(t):
                _w(t, S["lwl"], b["lwl"] + 32)
                _w(t, S["hid"], b["hid"])
                for h in range(HCH):
                    mm = t.matmul(psAXD[:, :], dasb[:, h * 16:(h + 1) * 16],
                                  hidden[:, h * 512:(h + 1) * 512],
                                  start=(h == 0), stop=(h == HCH - 1))
                mm.then_inc(S["axd2"].h, 1)
                for dm in range(DCH):
                    _w(t, S[f"fd{dm % 3}"], b[f"fd{dm % 3}"] + 16 * (dm // 3 + 1))
                    _w(t, S["gs"], b["gs"] + dm - 1)
                    if dm == 0:
                        _w(t, S["axe"], b["axe"] + 1)
                    wb_ = (dm % 3) * HCH * 128
                    pb = (dm % 2) * 512
                    for h in range(HCH):
                        t.matmul(psD[:, pb:pb + 512],
                                 dwb[:, wb_ + h * 128:wb_ + (h + 1) * 128],
                                 hidden[:, h * 512:(h + 1) * 512],
                                 start=(h == 0), stop=False)
                    t.matmul(psD[:, pb:pb + 512], dbsb[:, dm * 128:(dm + 1) * 128],
                             axdsb[:, :], start=False, stop=True
                             ).then_inc(S["dmm"].h, 1)
            S["axd2"].n += 1
            S["dmm"].n += DCH

            @blk.scalar
            def _(s):
                _w(s, S["axd2"], b["axd2"] + 1)
                s.copy(axdsb[:, :], psAXD[:, :]).then_inc(S["axe"].h, 1)
                for dm in range(DCH):
                    _w(s, S["dmm"], b["dmm"] + dm + 1)
                    _w(s, S["cmb"], b["cmb"] + dm - 1)
                    s.activation(gsb[:, (dm % 2) * 512:(dm % 2 + 1) * 512],
                                 psD[:, (dm % 2) * 512:(dm % 2 + 1) * 512],
                                 AF.Silu).then_inc(S["gs"].h, 1)
            S["axe"].n += 1
            S["gs"].n += DCH

            S["sx"].n += 32 * DCH

            @blk.vector
            def _(v):
                for dm in range(DCH):
                    _w(v, S["gs"], b["gs"] + dm + 1)
                    _w(v, S["sx"], b["sx"] + 32 * (dm + 1))
                    _w(v, S["outw"], b["outw"] + 16 * (dm - 1) if dm >= 2
                       else b["outw"])
                    bb = (dm % 2) * 512
                    v.tensor_tensor(tmb[:, :], ssb[:, bb:bb + 512],
                                    gsb[:, bb:bb + 512], ALU.mult)
                    v.tensor_tensor(osb[:, bb:bb + 512], tmb[:, :],
                                    xtb[:, bb:bb + 512], ALU.add
                                    ).then_inc(S["cmb"].h, 1)
            S["cmb"].n += DCH

            @blk.sync
            def _(sy):
                _w(sy, S["sw"], S["sw"].n)
                _w(sy, S["xtw"], S["xtw"].n)
                for dm in range(DCH):
                    _w(sy, S["cmb"], b["cmb"] + dm - 1)
                    sy.dma_start(ssb[:, (dm % 2) * 512:(dm % 2 + 1) * 512],
                                 s_scr[dm, :, tb * 512:(tb + 1) * 512]
                                 ).then_inc(S["sx"].h, 16)
                    sy.dma_start(xtb[:, (dm % 2) * 512:(dm % 2 + 1) * 512],
                                 xqt[dm * 128:(dm + 1) * 128,
                                     tb * 512:(tb + 1) * 512]
                                 ).then_inc(S["sx"].h, 16)
                    if dm >= 2:
                        sy.dma_start(out_t[(dm - 2) * 128:(dm - 1) * 128,
                                           tb * 512:(tb + 1) * 512],
                                     osb[:, ((dm - 2) % 2) * 512:
                                         ((dm - 2) % 2 + 1) * 512]
                                     ).then_inc(S["outw"].h, 16)
                for dm in (DCH - 2, DCH - 1):
                    _w(sy, S["cmb"], b["cmb"] + dm + 1)
                    sy.dma_start(out_t[dm * 128:(dm + 1) * 128,
                                       tb * 512:(tb + 1) * 512],
                                 osb[:, (dm % 2) * 512:(dm % 2 + 1) * 512]
                                 ).then_inc(S["outw"].h, 16)
            S["outw"].n += 16 * DCH

    nhalf = {"LN0": 1, "K0": 1, "Q": 0, "LNX": 0}.get(PHASES, 2)
    with nc.sbuf_tensor("xnt_half", [128, DCH * 2048], BF16) as xnt_half:
        for half in range(nhalf):
            ln_block(x, half * 2048, 16, xnt_half, 2048)
            if PHASES != "LN0":
                phase_k(half, xnt_half)
    if PHASES not in ("LN0", "K0", "K"):
        with nc.sbuf_tensor("xnt_own", [128, DCH * 1024], BF16) as xnt_own:
            for oh in range(1 if PHASES == "LNX" else 2):
                ln_block(xq, oh * 1024, 8, xnt_own, 1024)
                if PHASES != "LNX":
                    phase_q(oh, xnt_own)
    if PHASES in ("A", "ALL"):
        phase_a()
    if PHASES == "ALL":
        with (
            nc.sbuf_tensor("hidden", [128, HCH * 512], BF16) as hidden,
        ):
            for tb in range(NG):
                phase_f_a(tb, hidden)
                phase_f_b(tb, hidden)
        with nc.Block() as blk:
            @blk.sync
            def _(sy):
                sy.wait_ge(S["outw"].h, S["outw"].n)
    else:
        with nc.Block() as blk:
            final = [(S["kvw"], S["kvw"].n), (S["qw"], S["qw"].n),
                     (S["xnw"], S["xnw"].n), (S["xtw"], S["xtw"].n)]
            if PHASES == "A":
                final.append((S["sw"], S["sw"].n))

            @blk.sync
            def _(sy):
                for sem, val in final:
                    sy.wait_ge(sem.h, val)
    return nc


def _perm():
    p = np.empty(128, np.int64)
    for m in range(64):
        p[m] = 2 * m
        p[m + 64] = 2 * m + 1
    return p


def _rope_tables():
    inv = (1.0 / (10000.0 ** (np.arange(0, 128, 2, dtype=np.float32) / np.float32(128.0)))
           ).astype(np.float32)
    pos = np.arange(SEQ, dtype=np.float32)
    ang = (pos[:, None] * inv[None, :]).astype(np.float32)
    cos = np.cos(ang).astype(np.float32).T
    sin = np.sin(ang).astype(np.float32).T
    cos_t = np.concatenate([cos, cos], axis=0)
    sin_t = np.concatenate([-sin, sin], axis=0)
    return np.ascontiguousarray(cos_t), np.ascontiguousarray(sin_t)


def _prep_shared(Wq, Wk, Wv, Wo, gate_w, up_w, down_w, ga, gbm, ua, ubm, dam, dbm):
    perm = _perm()
    pc = np.concatenate([h * 128 + perm for h in range(NH)])
    Wq_p = Wq[:, pc]
    pk = np.concatenate([h * 128 + perm for h in range(NKV)])
    Wk_p = Wk[:, pk]
    o = {}
    o["wq"] = np.ascontiguousarray(Wq_p.reshape(DCH, 128, NH, 128).transpose(2, 1, 0, 3))
    o["wk"] = np.ascontiguousarray(Wk_p.reshape(DCH, 128, NKV, 128).transpose(2, 1, 0, 3))
    o["wv"] = np.ascontiguousarray(Wv.reshape(DCH, 128, 4, 256).transpose(2, 1, 0, 3))
    o["wo"] = np.ascontiguousarray(Wo.reshape(DCH, 128, DCH, 128).transpose(2, 1, 0, 3))
    o["gw"] = np.ascontiguousarray(gate_w.reshape(HCH, 128, DCH, 128).transpose(0, 3, 2, 1))
    o["uw"] = np.ascontiguousarray(up_w.reshape(HCH, 128, DCH, 128).transpose(0, 3, 2, 1))
    o["dw"] = np.ascontiguousarray(down_w.reshape(DCH, 128, HCH, 128).transpose(0, 3, 2, 1))
    A = np.concatenate([ga, ua], axis=0)
    o["gua"] = np.ascontiguousarray(A.reshape(32, DCH, 128).transpose(2, 1, 0))
    o["gb"] = np.ascontiguousarray(gbm.reshape(HCH, 128, 16).transpose(2, 0, 1))
    o["ub"] = np.ascontiguousarray(ubm.reshape(HCH, 128, 16).transpose(2, 0, 1))
    o["da"] = np.ascontiguousarray(dam.reshape(16, HCH, 128).transpose(2, 1, 0))
    o["db"] = np.ascontiguousarray(dbm.reshape(DCH, 128, 16).transpose(2, 0, 1))
    o["idbf"] = np.eye(128, dtype=BF)
    o["idf32"] = np.eye(128, dtype=np.float32)
    return o


def _core_js(half):
    return [7, 5, 3, 1] if half == 0 else [6, 4, 2, 0]


def _masks_for(js):
    m = np.zeros((NG, 8, 128, GSZ), dtype=BF)
    for g in range(NG):
        j = js[g]
        for i in range(8):
            t = CNT[g] - 8 + i
            kp = t * 128 + np.arange(128)
            qp = j * 512 + np.arange(GSZ)
            m[g, i] = (kp[:, None] <= qp[None, :]).astype(BF)
    return m


def kernel(L0_attn_x, L0_attn_Wq, L0_attn_Wk, L0_attn_Wv, L0_attn_Wo,
           L0_ffn_gate_w, L0_ffn_up_w, L0_ffn_down_w,
           L0_ffn_gate_a, L0_ffn_gate_b, L0_ffn_up_a, L0_ffn_up_b,
           L0_ffn_down_a, L0_ffn_down_b):
    xf = np.asarray(L0_attn_x, dtype=np.float32)
    shared = _prep_shared(
        np.asarray(L0_attn_Wq, np.float32), np.asarray(L0_attn_Wk, np.float32),
        np.asarray(L0_attn_Wv, np.float32), np.asarray(L0_attn_Wo, np.float32),
        np.asarray(L0_ffn_gate_w, np.float32), np.asarray(L0_ffn_up_w, np.float32),
        np.asarray(L0_ffn_down_w, np.float32),
        np.asarray(L0_ffn_gate_a, np.float32), np.asarray(L0_ffn_gate_b, np.float32),
        np.asarray(L0_ffn_up_a, np.float32), np.asarray(L0_ffn_up_b, np.float32),
        np.asarray(L0_ffn_down_a, np.float32), np.asarray(L0_ffn_down_b, np.float32))
    cos_t, sin_t = _rope_tables()
    shared["kcos"] = cos_t
    shared["ksin"] = sin_t

    in_maps = []
    for c in range(8):
        bat, half = c // 2, c % 2
        js = _core_js(half)
        qpos = np.concatenate([np.arange(j * 512, (j + 1) * 512) for j in js])
        m = dict(shared)
        m["x"] = np.ascontiguousarray(xf[bat])
        m["xq"] = np.ascontiguousarray(xf[bat][qpos])
        m["xqt"] = np.ascontiguousarray(xf[bat][qpos].T)
        m["qcos"] = np.ascontiguousarray(cos_t[:, qpos])
        m["qsin"] = np.ascontiguousarray(sin_t[:, qpos])
        m["msk"] = _masks_for(js)
        in_maps.append(m)

    nc = build_nc()
    res = run_bass_kernel_spmd(nc, in_maps, core_ids=list(range(8)))

    out = np.empty((4, SEQ, D), dtype=np.float32)
    for c in range(8):
        bat, half = c // 2, c % 2
        js = _core_js(half)
        ot = np.ascontiguousarray(np.asarray(res.results[c]["out_t"]).T)
        for g in range(NG):
            j = js[g]
            out[bat, j * 512:(j + 1) * 512, :] = ot[g * 512:(g + 1) * 512, :]
    if DEBUG:
        kernel._dbg = res
    return out
